# version 15
# baseline (speedup 1.0000x reference)
import sys
sys.path.insert(0, "/opt/trn_rl_repo")
from contextlib import ExitStack
import numpy as np
import ml_dtypes
import concourse.bass as bass
import concourse.mybir as mybir
from concourse.bass_utils import run_bass_kernel_spmd

F32 = mybir.dt.float32
BF16 = mybir.dt.bfloat16
AF = mybir.ActivationFunctionType
ALU = mybir.AluOpType
AX = mybir.AxisListType
f32 = np.float32

EPS = 1e-5
MB = 20.0


class P:
    def __init__(self, nc):
        self.nc = nc
        self.sem = {}
        self.cnt = {}

    def new_sem(self, es, name):
        self.sem[name] = es.enter_context(self.nc.semaphore(name))
        self.cnt[name] = 0

    def inc(self, ins, name, n=1):
        self.cnt[name] += n
        ins.then_inc(self.sem[name], n)
        return ins

    def w(self, eng, name, val):
        if val > 0:
            eng.wait_ge(self.sem[name], val)


def build_nc(T=2048, H=16, KVH=4, HF=8192, stop_after=None):
    D = 128
    C = H * D
    G = H // KVH
    NCC = C // 128
    NT = T // 128
    NHC = HF // 128
    HBS = min(16, NHC)
    NHB = NHC // HBS
    TH = T // 2
    SW = min(512, T)
    SWH = min(512, T // 2)
    SWT = min(512, TH)
    NS = T // SW
    NS2 = (T // 2) // SWH
    NSH = TH // SWT
    KVD = KVH * D
    NU = NT * H
    NKV = NT * KVH
    assert T % 256 == 0 and H % KVH == 0 and NHC % HBS == 0 and NCC == H

    nc = bass.Bass("TRN2", target_bir_lowering=False)

    xT = nc.dram_tensor("xT", [C, T], F32, kind="ExternalInput").ap()
    wqT = nc.dram_tensor("wqT", [C, C], F32, kind="ExternalInput").ap()
    wkT = nc.dram_tensor("wkT", [C, KVD], F32, kind="ExternalInput").ap()
    wvT = nc.dram_tensor("wvT", [C, KVD], F32, kind="ExternalInput").ap()
    woT = nc.dram_tensor("woT", [C, C], F32, kind="ExternalInput").ap()
    p1T = nc.dram_tensor("p1T", [C, HF], F32, kind="ExternalInput").ap()
    p2T = nc.dram_tensor("p2T", [HF, C], F32, kind="ExternalInput").ap()
    bq_h = nc.dram_tensor("bq_fm", [128, H], F32, kind="ExternalInput").ap()
    bk_h = nc.dram_tensor("bk_fm", [128, KVH], F32, kind="ExternalInput").ap()
    bv_h = nc.dram_tensor("bv_row", [1, KVD], F32, kind="ExternalInput").ap()
    bo_h = nc.dram_tensor("bo_fm", [128, NCC], F32, kind="ExternalInput").ap()
    b1_h = nc.dram_tensor("b1_fm", [128, NHC], F32, kind="ExternalInput").ap()
    b2_h = nc.dram_tensor("b2_fm", [128, NCC], F32, kind="ExternalInput").ap()
    cos_h = nc.dram_tensor("cos_t", [128, H], F32, kind="ExternalInput").ap()
    ssin_h = nc.dram_tensor("ssin_t", [128, H], F32, kind="ExternalInput").ap()
    etril_h = nc.dram_tensor("etril", [128, 128], F32, kind="ExternalInput").ap()
    ident_h = nc.dram_tensor("ident", [128, 128], F32, kind="ExternalInput").ap()
    gate_h = nc.dram_tensor("gate_w", [128, (C // 128) * 8], F32,
                            kind="ExternalInput").ap()
    r1T = nc.dram_tensor("r1T", [C, T // 2], F32, kind="Internal").ap()
    outT = nc.dram_tensor("outT", [C, T], F32, kind="ExternalOutput").ap()

    p = P(nc)
    es0 = ExitStack()
    with es0:
        for s in ["ld_c", "ld_x0", "ld_x1", "ld_xg", "ld_w0", "ld_w1",
                  "ld_p10", "ld_p11", "ld_p20", "ld_p21", "ld_r1",
                  "st_r10", "st_r11", "st_o0", "st_o1",
                  "s_pool", "s_pe", "s_act", "s_dve",
                  "s_pe_sc", "s_pe_tr", "s_pe_pv",
                  "s_act_exp", "s_act_af", "s_dve_c", "s_dve_p", "s_dve_ev"]:
            p.new_sem(es0, s)

        SB = lambda n, shp, dt: es0.enter_context(nc.sbuf_tensor(n, shp, dt))

        etril = SB("etril_s", [128, 128], F32)
        ident = SB("ident_s", [128, 128], BF16)
        identf = SB("identf_s", [128, 128], F32)
        cos_s = SB("cos_s", [128, H], F32)
        ssin_s = SB("ssin_s", [128, H], F32)
        bqf = SB("bqf", [128, H], F32)
        bkf = SB("bkf", [128, KVH], F32)
        bvr = SB("bvr", [1, KVD], F32)
        bvr_bf = SB("bvr_bf", [1, KVD], BF16)
        ones1 = SB("ones1", [1, 128], BF16)
        ones128 = SB("ones128", [128, 128], BF16)
        bof = SB("bof", [128, NCC], F32)
        b1f = SB("b1f", [128, NHC], F32)
        b2f = SB("b2f", [128, NCC], F32)
        bqcos = SB("bqcos", [128, H], F32)
        bqssin = SB("bqssin", [128, H], F32)
        bkcos = SB("bkcos", [128, KVH], F32)
        bkssin = SB("bkssin", [128, KVH], F32)
        shq = SB("shq", [128, H], F32)
        shk = SB("shk", [128, KVH], F32)
        gate_s = SB("gate_s", [128, (C // 128) * 8], F32)
        Zp = [SB(f"Zp{i}", [128, 8], F32) for i in range(2)]
        Zs = [SB(f"Zs{i}", [128, 1], F32) for i in range(2)]
        rz = [SB(f"rz{i}", [128, 1], F32) for i in range(2)]

        esBIG = ExitStack()
        qbuf = esBIG.enter_context(nc.sbuf_tensor("qbuf", [128, H * T], BF16))
        xn1 = esBIG.enter_context(nc.sbuf_tensor("xn1", [128, NCC * T], BF16))
        xb = qbuf
        attfm = qbuf
        p_sb = [lambda i=i: xn1[:, i * T:(i + 1) * T] for i in range(3)]
        pTg_of = lambda sl, g: xn1[:, (3 + sl * G + g) * T:(4 + sl * G + g) * T]

        esB2 = ExitStack()

        with nc.Block() as block:
            def full_barrier(skip_sync=False):
                tgt = {k: v for k, v in p.cnt.items() if v > 0}
                regs = [block.gpsimd, block.tensor, block.scalar, block.vector]
                if not skip_sync:
                    regs.append(block.sync)
                for reg in regs:
                    def _bar(eng, tgt=tgt):
                        for k, v in tgt.items():
                            eng.wait_ge(p.sem[k], v)
                    reg(_bar)

            def ph0_sync(sync):
                for dst, src in [(etril[:], etril_h), (identf[:], ident_h),
                                 (gate_s[:], gate_h),
                                 (cos_s[:], cos_h), (ssin_s[:], ssin_h),
                                 (bqf[:], bq_h), (bkf[:], bk_h), (bvr[:], bv_h),
                                 (bof[:], bo_h), (b1f[:], b1_h), (b2f[:], b2_h)]:
                    p.inc(sync.dma_start(out=dst, in_=src), "ld_c", 16)
            block.sync(ph0_sync)

            def ph0_pool(g):
                p.w(g, "ld_c", p.cnt["ld_c"])
                g.memset(ones1[:], 1.0)
                g.memset(ones128[:], 1.0)
                g.tensor_copy(ident[:], identf[:])
                p.inc(g.tensor_copy(bvr_bf[:], bvr[:]), "s_pool")
            block.gpsimd(ph0_pool)

            def ph0_dve(v):
                p.w(v, "ld_c", p.cnt["ld_c"])
                v.tensor_mul(bqcos[:], bqf[:], cos_s[:])
                v.tensor_copy(shq[0:64, :], bqf[64:128, :])
                p.inc(v.tensor_copy(shq[64:128, :], bqf[0:64, :]), "s_dve")
                p.w(v, "s_dve", p.cnt["s_dve"])
                v.tensor_mul(bqssin[:], shq[:], ssin_s[:])
                v.tensor_mul(bkcos[:], bkf[:], cos_s[:, 0:KVH])
                v.tensor_copy(shk[0:64, :], bkf[64:128, :])
                p.inc(v.tensor_copy(shk[64:128, :], bkf[0:64, :]), "s_dve")
                p.w(v, "s_dve", p.cnt["s_dve"])
                p.inc(v.tensor_mul(bkssin[:], shk[:], ssin_s[:, 0:KVH]), "s_dve")
            block.vector(ph0_dve)

            esW = ExitStack()
            wstg = [esW.enter_context(nc.sbuf_tensor(f"wstg{i}", [128, C], F32))
                    for i in range(2)]
            wbf = [esW.enter_context(nc.sbuf_tensor(f"wbf{i}", [128, C], BF16))
                   for i in range(2)]

            full_barrier()
            with ExitStack() as esA:
                A = lambda n, shp, dt: esA.enter_context(nc.sbuf_tensor(n, shp, dt))
                xstg = [A(f"xstg{i}", [128, T], F32) for i in range(2)]
                sq = [A(f"sq{i}", [128, T], BF16) for i in range(2)]
                mb_b = A("mb_b", [128, T], F32)
                t2_b = A("t2_b", [128, T], F32)
                ab_b = A("ab_b", [128, T], BF16)
                bma_b = A("bma_b", [128, T], BF16)
                psS = esA.enter_context(nc.psum_tensor("psS", [128, T], F32))
                psSS = esA.enter_context(nc.psum_tensor("psSS", [128, T], F32))
                bA = dict(p.cnt)

                def phA_sync(sync):
                    for c in range(0, NCC, 2):
                        p.w(sync, "s_pool", bA["s_pool"] + c - 1)
                        p.w(sync, "s_act", bA["s_act"] + c - 1)
                        p.inc(sync.dma_start(out=xstg[0][:],
                                             in_=xT[c * 128:(c + 1) * 128, :]),
                              "ld_x0", 16)
                block.sync(phA_sync)

                def phA_syncb(g):
                    for c in range(1, NCC, 2):
                        p.w(g, "s_pool", bA["s_pool"] + c - 1)
                        p.w(g, "s_act", bA["s_act"] + c - 1)
                        p.inc(g.dma_start(out=xstg[1][:],
                                          in_=xT[c * 128:(c + 1) * 128, :]),
                              "ld_xg", 16)
                block.gpsimd(phA_syncb)

                def phA_v0(v):
                    for c in range(NCC):
                        p.w(v, "ld_x0" if c % 2 == 0 else "ld_xg",
                            16 * (c // 2 + 1))
                        p.inc(v.tensor_copy(xb[:, c * T:(c + 1) * T],
                                            xstg[c % 2][:]), "s_pool")
                block.vector(phA_v0)

                def phA_sq(a):
                    for c in range(NCC):
                        p.w(a, "ld_x0" if c % 2 == 0 else "ld_xg",
                            16 * (c // 2 + 1))
                        p.w(a, "s_pe", bA["s_pe"] + c - 1)
                        p.inc(a.activation(sq[c % 2][:], xstg[c % 2][:],
                                           AF.Square), "s_act")
                block.scalar(phA_sq)

                def phA_pe(t):
                    for c in range(NCC):
                        p.w(t, "s_pool", bA["s_pool"] + c + 1)
                        p.w(t, "s_act", bA["s_act"] + c + 1)
                        for ns in range(NS):
                            sl = slice(ns * SW, (ns + 1) * SW)
                            t.matmul(psS[:, sl], ones128[:],
                                     xb[:, c * T + ns * SW:c * T + (ns + 1) * SW],
                                     start=(c == 0), stop=(c == NCC - 1))
                            ins = t.matmul(psSS[:, sl], ones128[:], sq[c % 2][:, sl],
                                           start=(c == 0), stop=(c == NCC - 1))
                        p.inc(ins, "s_pe")
                block.tensor(phA_pe)

                def phA_v1(v):
                    p.w(v, "s_pe", bA["s_pe"] + NCC)
                    p.inc(v.tensor_scalar(out=mb_b[:], in0=psS[:], scalar1=1.0 / C,
                                          scalar2=None, op0=ALU.mult), "s_dve")
                block.vector(phA_v1)

                def phA_s1(s):
                    p.w(s, "s_dve", p.cnt["s_dve"])
                    p.inc(s.activation(t2_b[:], mb_b[:], AF.Square), "s_act")
                block.scalar(phA_s1)

                def phA_v2(v):
                    p.w(v, "s_act", p.cnt["s_act"])
                    p.inc(v.scalar_tensor_tensor(out=t2_b[:], in0=psSS[:],
                                                 scalar=1.0 / C, in1=t2_b[:],
                                                 op0=ALU.mult, op1=ALU.subtract),
                          "s_dve")
                block.vector(phA_v2)

                def phA_s2(s):
                    p.w(s, "s_dve", p.cnt["s_dve"])
                    p.inc(s.activation(t2_b[:], t2_b[:], AF.Ln, bias=f32(EPS)),
                          "s_act")
                    p.w(s, "s_act", p.cnt["s_act"])
                    p.inc(s.activation(ab_b[:], t2_b[:], AF.Exp, scale=f32(-0.5)),
                          "s_act")
                block.scalar(phA_s2)

                def phA_v3(v):
                    p.w(v, "s_act", p.cnt["s_act"])
                    p.inc(v.scalar_tensor_tensor(out=bma_b[:], in0=mb_b[:],
                                                 scalar=-1.0, in1=ab_b[:],
                                                 op0=ALU.mult, op1=ALU.mult),
                          "s_dve")
                    for c in range(NCC):
                        cs = slice(c * T, (c + 1) * T)
                        p.w(v, "s_dve", p.cnt["s_dve"])
                        p.inc(v.tensor_mul(xn1[:, cs], xb[:, cs], ab_b[:]),
                              "s_dve")
                        p.w(v, "s_dve", p.cnt["s_dve"])
                        p.inc(v.tensor_add(xn1[:, cs], xn1[:, cs], bma_b[:]),
                              "s_dve")
                block.vector(phA_v3)

            b1ev = [0]
            kbuf = esB2.enter_context(
                nc.sbuf_tensor("kbuf", [128, KVH * T], BF16))
            vTb = esB2.enter_context(
                nc.sbuf_tensor("vTb", [128, NT * KVD], BF16))
            full_barrier(skip_sync=True)
            with ExitStack() as esB:
                Bm = lambda n, shp, dt: esB.enter_context(nc.sbuf_tensor(n, shp, dt))

                with ExitStack() as esB0:
                    wvbf = esB0.enter_context(
                        nc.sbuf_tensor("wvbf", [128, NCC * KVD], BF16))
                    psvt = [esB0.enter_context(
                        nc.psum_tensor(f"psvt{i}", [128, KVD], F32))
                        for i in range(2)]
                    bB0 = dict(p.cnt)
                    CPS = (128 * C) // (128 * KVD)
                    NQ = NCC // CPS

                    def phB0_sync(sync):
                        for q in range(NQ):
                            p.w(sync, "s_pool", bB0["s_pool"] + q - 1)
                            p.inc(sync.dma_start(
                                out=wstg[q % 2][:, 0:CPS * KVD]
                                .rearrange("p (c i) -> p c i", i=KVD),
                                in_=wvT[q * CPS * 128:(q + 1) * CPS * 128, :]
                                .rearrange("(c p) i -> p c i", p=128)),
                                f"ld_w{q % 2}", 16)
                    block.sync(phB0_sync)

                    def phB0_cast(v):
                        for q in range(NQ):
                            p.w(v, f"ld_w{q % 2}", 16 * (q // 2 + 1))
                            p.inc(v.tensor_copy(
                                wvbf[:, q * CPS * KVD:(q + 1) * CPS * KVD],
                                wstg[q % 2][:, 0:CPS * KVD]), "s_pool")
                    block.vector(phB0_cast)

                    def phB0_pe(t):
                        p.w(t, "s_dve", bB0["s_dve"])
                        p.w(t, "s_pool", bB0["s_pool"] + NQ)
                        for j in range(NT):
                            p.w(t, "s_act", bB0["s_act"] + j - 1)
                            for c in range(NCC):
                                t.matmul(psvt[j % 2][:],
                                         xn1[:, c * T + j * 128:c * T + (j + 1) * 128],
                                         wvbf[:, c * KVD:(c + 1) * KVD],
                                         start=(c == 0), stop=False)
                            p.inc(t.matmul(psvt[j % 2][:], ones1[:], bvr_bf[:],
                                           start=False, stop=True), "s_pe")
                    block.tensor(phB0_pe)

                    def phB0_act(s):
                        for j in range(NT):
                            p.w(s, "s_pe", bB0["s_pe"] + j + 1)
                            p.inc(s.activation(vTb[:, j * KVD:(j + 1) * KVD],
                                               psvt[j % 2][:], AF.Identity,
                                               bias=f32(0.0)), "s_act")
                    block.scalar(phB0_act)

                full_barrier(skip_sync=True)
                with ExitStack() as esB1:
                    B1 = lambda n, shp, dt: esB1.enter_context(
                        nc.sbuf_tensor(n, shp, dt))
                    qc_b = B1("qc_b", [128, T], F32)
                    tmp_b = B1("tmp_b", [128, T], F32)
                    psqk = [esB1.enter_context(
                        nc.psum_tensor(f"psqk{i}", [128, T], F32))
                        for i in range(2)]
                    bB = dict(p.cnt)
                    ldw_b = [p.cnt["ld_w0"], p.cnt["ld_w1"]]
                    NOC = KVH + H

                    def phB_sync(sync):
                        for oc in range(NOC):
                            src, idx = (wkT, oc) if oc < KVH else (wqT, oc - KVH)
                            p.w(sync, "s_pool", bB["s_pool"] + oc - 1)
                            p.inc(sync.dma_start(
                                out=wstg[oc % 2][:]
                                .rearrange("p (c i) -> p c i", i=128),
                                in_=src[:, idx * 128:(idx + 1) * 128]
                                .rearrange("(c p) i -> p c i", p=128)),
                                f"ld_w{oc % 2}", 16)
                    block.sync(phB_sync)


                    b1ev[0] = bB["s_dve"]

                    def phB_pe(t):
                        p.w(t, "s_act", bB["s_act"])
                        for oc in range(NOC):
                            p.w(t, "s_pool", bB["s_pool"] + oc + 1)
                            p.w(t, "s_act", bB["s_act"] + oc - 1)
                            for c in range(NCC):
                                for ns in range(NS):
                                    ins = t.matmul(
                                        psqk[oc % 2][:, ns * SW:(ns + 1) * SW],
                                        wbf[oc % 2][:, c * 128:(c + 1) * 128],
                                        xn1[:, c * T + ns * SW:
                                            c * T + (ns + 1) * SW],
                                        start=(c == 0), stop=(c == NCC - 1))
                            p.inc(ins, "s_pe")
                    block.tensor(phB_pe)

                    def phB_act(s):
                        for oc in range(NOC):
                            isk = oc < KVH
                            idx = oc if isk else oc - KVH
                            ct = bkcos if isk else bqcos
                            st_ = bkssin if isk else bqssin
                            p.w(s, "s_pe", bB["s_pe"] + oc + 1)
                            p.w(s, "s_dve", bB["s_dve"] + oc)
                            s.activation(qc_b[:], psqk[oc % 2][:], AF.Identity,
                                         bias=ct[:, idx:idx + 1],
                                         scale=cos_s[:, idx:idx + 1])
                            s.activation(tmp_b[0:64, :], psqk[oc % 2][64:128, :],
                                         AF.Identity,
                                         bias=st_[0:64, idx:idx + 1],
                                         scale=ssin_s[0:64, idx:idx + 1])
                            p.inc(s.activation(tmp_b[64:128, :],
                                               psqk[oc % 2][0:64, :], AF.Identity,
                                               bias=st_[64:128, idx:idx + 1],
                                               scale=ssin_s[64:128, idx:idx + 1]),
                                  "s_act")
                    block.scalar(phB_act)

                    def phB_dve(v):
                        def evac(oc):
                            isk = oc < KVH
                            idx = oc if isk else oc - KVH
                            dst = (kbuf[:, idx * T:(idx + 1) * T] if isk
                                   else qbuf[:, idx * T:(idx + 1) * T])
                            p.w(v, "s_act", bB["s_act"] + oc + 1)
                            p.inc(v.tensor_add(dst, qc_b[:], tmp_b[:]), "s_dve")
                        for oc in range(NOC):
                            p.w(v, f"ld_w{oc % 2}",
                                ldw_b[oc % 2] + 16 * (oc // 2 + 1))
                            p.w(v, "s_pe", bB["s_pe"] + oc - 1)
                            p.inc(v.tensor_copy(wbf[oc % 2][:], wstg[oc % 2][:]),
                                  "s_pool")
                            if oc >= 1:
                                evac(oc - 1)
                        evac(NOC - 1)
                    block.vector(phB_dve)

            if stop_after == "B":
                esB2.close(); esBIG.close()
                return nc
            full_barrier(skip_sync=True)
            with ExitStack() as esC:
                sc = [esC.enter_context(
                    nc.psum_tensor(f"sc{i}", [128, T // 2], F32))
                    for i in range(2)]
                ps_t = esC.enter_context(nc.psum_tensor("ps_t", [128, T], BF16))
                pvps = [esC.enter_context(
                    nc.psum_tensor(f"pvps{i}", [128, G * 128], F32))
                    for i in range(2)]
                bC = dict(p.cnt)

                def ranges_of(tt):
                    b = tt * 128
                    r = []
                    if b > 0:
                        r.append((0, b, -MB + 1.0))
                    r.append((b, T, -MB))
                    return r

                def n_accum_cols(tt):
                    n = 0
                    for sh in range(2):
                        h0, h1 = sh * (T // 2), (sh + 1) * (T // 2)
                        for (lo, hi, _bias) in ranges_of(tt):
                            if max(lo, h0) < min(hi, h1):
                                n += 1
                    return n

                def phC_pe(t):
                    p.w(t, "s_act", bC["s_act"])
                    for u in range(NU + 3):
                        if u < NU:
                            tt, hh = divmod(u, H)
                            kv = hh // G
                            if tt == 0:
                                p.w(t, "s_dve", b1ev[0] + KVH + hh + 1)
                            lhs = qbuf[:, hh * T + tt * 128:hh * T + (tt + 1) * 128]
                            p.w(t, "s_act_exp", 2 * u)
                            for sh in range(2):
                                for n2 in range(NS2):
                                    o = kv * T + sh * (T // 2) + n2 * SWH
                                    ins = t.matmul(
                                        sc[sh][:, n2 * SWH:(n2 + 1) * SWH],
                                        lhs, kbuf[:, o:o + SWH],
                                        start=True, stop=True)
                                p.inc(ins, "s_pe_sc")
                        if 2 <= u < NU + 2:
                            up = u - 2
                            p.w(t, "s_dve_p", up + 1)
                            p.w(t, "s_dve_ev", up)
                            for j in range(NT):
                                ins = t.transpose(
                                    ps_t[:, j * 128:(j + 1) * 128],
                                    p_sb[up % 3]()[:, j * 128:(j + 1) * 128],
                                    ident[:])
                            p.inc(ins, "s_pe_tr")
                        if u >= 6 and (u - 6) % G == 0 and (u - 6) // G < NKV:
                            kvi = (u - 6) // G
                            tt2, kv2 = divmod(kvi, KVH)
                            p.w(t, "s_dve_ev", 4 * kvi + 4)
                            p.w(t, "s_act_af", kvi - 1)
                            for j in range(NT):
                                lv = vTb[:, j * KVD + kv2 * 128:
                                         j * KVD + (kv2 + 1) * 128]
                                for g in range(G):
                                    ins = t.matmul(
                                        pvps[kvi % 2][:, g * 128:(g + 1) * 128],
                                        lv,
                                        pTg_of(kvi % 2, g)[:, j * 128:
                                                           (j + 1) * 128],
                                        start=(j == 0 and g == 0),
                                        stop=(j == NT - 1 and g == G - 1))
                            p.inc(ins, "s_pe_pv")
                block.tensor(phC_pe)

                def emit_af(s, kvi):
                    tt2, kv2 = divmod(kvi, KVH)
                    p.w(s, "s_pe_pv", kvi + 1)
                    dst = attfm[:].rearrange("p (h t) -> p h t", t=T)[
                        :, kv2 * G:(kv2 + 1) * G, tt2 * 128:(tt2 + 1) * 128]
                    src = pvps[kvi % 2][:].rearrange("p (g t) -> p g t", t=128)
                    p.inc(s.activation(dst, src, AF.Identity, bias=f32(0.0)),
                          "s_act_af")

                def phC_act(s):
                    for u in range(NU):
                        tt, hh = divmod(u, H)
                        rs = ranges_of(tt)
                        ncol = 0
                        p.w(s, "s_pe_sc", 2 * u + 2)
                        p.w(s, "s_dve_p", u - 1)
                        for sh in range(2):
                            h0, h1 = sh * (T // 2), (sh + 1) * (T // 2)
                            last = None
                            for (lo, hi, bias) in rs:
                                il, ih = max(lo, h0), min(hi, h1)
                                if il >= ih:
                                    continue
                                acc = Zp[u % 2][:, ncol:ncol + 1]
                                ncol += 1
                                last = s.activation(
                                    p_sb[u % 3]()[:, il:ih],
                                    sc[sh][:, il - h0:ih - h0], AF.Exp,
                                    bias=f32(bias), scale=f32(D ** -0.5),
                                    accum_out=acc)
                            p.inc(last, "s_act_exp")
                        if u >= 7 and (u - 7) % G == 0 and (u - 7) // G < NKV:
                            emit_af(s, (u - 7) // G)
                    for kvi in range(((NU - 8) // G) + 1, NKV):
                        emit_af(s, kvi)
                block.scalar(phC_act)

                def phC_dve(v):
                    for u in range(NU + 2):
                        if u >= 2:
                            ue = u - 2
                            kvi, g = divmod(ue, G)
                            p.w(v, "s_pe_tr", ue + 1)
                            if g == 0:
                                p.w(v, "s_pe_pv", kvi - 1)
                            p.inc(v.tensor_copy(pTg_of(kvi % 2, g)[:], ps_t[:]),
                                  "s_dve_ev")
                        if u < NU:
                            tt, hh = divmod(u, H)
                            ncol = n_accum_cols(tt)
                            dlo = tt * 128
                            pd = p_sb[u % 3]()[:, dlo:dlo + 128]
                            p.w(v, "s_act_exp", 2 * u + 2)
                            p.inc(v.reduce_sum(Zp[u % 2][:, ncol:ncol + 1], pd,
                                               axis=AX.X, negate=True), "s_dve_c")
                            p.w(v, "s_dve_c", 4 * u + 1)
                            p.inc(v.scalar_tensor_tensor(
                                out=pd, in0=pd, scalar=1.0, in1=etril[:],
                                op0=ALU.mult, op1=ALU.mult,
                                accum_out=Zp[u % 2][:, ncol + 1:ncol + 2]),
                                "s_dve_c")
                            p.w(v, "s_dve_c", 4 * u + 2)
                            p.inc(v.reduce_sum(Zs[u % 2][:],
                                               Zp[u % 2][:, 0:ncol + 2],
                                               axis=AX.X), "s_dve_c")
                            p.w(v, "s_dve_c", 4 * u + 3)
                            p.inc(v.reciprocal(rz[u % 2][:], Zs[u % 2][:]),
                                  "s_dve_c")
                            p.w(v, "s_dve_c", 4 * u + 4)
                            p.inc(v.tensor_scalar(out=p_sb[u % 3]()[:],
                                                  in0=p_sb[u % 3]()[:],
                                                  scalar1=rz[u % 2][:],
                                                  scalar2=None,
                                                  op0=ALU.mult), "s_dve_p")
                block.vector(phC_dve)
            esB2.close()

            if stop_after == "C":
                esBIG.close()
                return nc
            full_barrier(skip_sync=True)
            with ExitStack() as esD:
                Dm = lambda n, shp, dt: esD.enter_context(nc.sbuf_tensor(n, shp, dt))
                xblk = [Dm(f"xblk{i}", [128, T], F32) for i in range(2)]
                tmpD = Dm("tmpD", [128, T], F32)
                r1b = [Dm(f"r1b{i}", [128, TH], F32) for i in range(2)]
                r1acc = xn1[:].bitcast(F32)
                pso = [esD.enter_context(nc.psum_tensor(f"pso{i}", [128, T], F32))
                       for i in range(2)]
                bD = dict(p.cnt)
                ldw_bD = [p.cnt["ld_w0"], p.cnt["ld_w1"]]
                ldx_bD = [p.cnt["ld_x0"], p.cnt["ld_x1"]]

                def phD_sync(sync):
                    def wdma(i):
                        p.w(sync, "s_pool", bD["s_pool"] + i - 1)
                        p.inc(sync.dma_start(
                            out=wstg[i % 2][:].rearrange("p (c i) -> p c i", i=128),
                            in_=woT[:, i * 128:(i + 1) * 128]
                                .rearrange("(c p) i -> p c i", p=128)),
                            f"ld_w{i % 2}", 16)
                    wdma(0)
                    wdma(1)
                    for i in range(NCC):
                        if i == 0:
                            p.w(sync, "s_pe_pv", NKV)
                        p.w(sync, "s_dve", bD["s_dve"] + i - 1)
                        p.inc(sync.dma_start(out=xblk[i % 2][:],
                                             in_=xT[i * 128:(i + 1) * 128, :]),
                              f"ld_x{i % 2}", 16)
                        if i + 2 < NCC:
                            wdma(i + 2)
                block.sync(phD_sync)

                def phD_pool(g):
                    for i in range(NCC):
                        p.w(g, "s_dve", bD["s_dve"] + i + 1)
                        p.inc(g.dma_start(out=r1T[i * 128:(i + 1) * 128, :],
                                          in_=r1b[i % 2][:]), f"st_r1{i % 2}", 16)
                block.gpsimd(phD_pool)

                def phD_pe(t):
                    p.w(t, "s_dve_ev", NU)
                    p.w(t, "s_act_af", NKV)
                    for i in range(NCC):
                        p.w(t, "s_pool", bD["s_pool"] + i + 1)
                        p.w(t, "s_act", bD["s_act"] + i - 1)
                        for ca in range(NCC):
                            for ns in range(NS):
                                ins = t.matmul(
                                    pso[i % 2][:, ns * SW:(ns + 1) * SW],
                                    wbf[i % 2][:, ca * 128:(ca + 1) * 128],
                                    attfm[:, ca * T + ns * SW:
                                          ca * T + (ns + 1) * SW],
                                    start=(ca == 0), stop=(ca == NCC - 1))
                        p.inc(ins, "s_pe")
                block.tensor(phD_pe)

                def phD_act(s):
                    for i in range(NCC):
                        p.w(s, "s_pe", bD["s_pe"] + i + 1)
                        p.w(s, "s_dve", bD["s_dve"] + i)
                        p.inc(s.activation(tmpD[:], pso[i % 2][:], AF.Identity,
                                           bias=bof[:, i:i + 1]), "s_act")
                block.scalar(phD_act)

                def phD_dve(v):
                    def radd(i):
                        p.w(v, "s_act", bD["s_act"] + i + 1)
                        p.w(v, f"ld_x{i % 2}", ldx_bD[i % 2] + 16 * (i // 2 + 1))
                        p.w(v, f"st_r1{i % 2}", 16 * (i // 2))
                        v.tensor_add(r1acc[:, i * TH:(i + 1) * TH],
                                     tmpD[:, 0:TH], xblk[i % 2][:, 0:TH])
                        p.inc(v.tensor_add(r1b[i % 2][:], tmpD[:, TH:T],
                                           xblk[i % 2][:, TH:T]), "s_dve")
                    for i in range(NCC):
                        p.w(v, f"ld_w{i % 2}", ldw_bD[i % 2] + 16 * (i // 2 + 1))
                        p.w(v, "s_pe", bD["s_pe"] + i - 1)
                        p.inc(v.tensor_copy(wbf[i % 2][:], wstg[i % 2][:]),
                              "s_pool")
                        if i >= 1:
                            radd(i - 1)
                    radd(NCC - 1)
                block.vector(phD_dve)


            esW.close()
            full_barrier()
            with ExitStack() as esE:
                Em = lambda n, shp, dt: esE.enter_context(nc.sbuf_tensor(n, shp, dt))
                r1acc = xn1[:].bitcast(F32)
                xn2 = qbuf[:, 0:NCC * TH]
                hblk = qbuf[:, NCC * TH:NCC * TH + HBS * TH]
                p1stg = [Em(f"p1stg{i}", [128, C], F32) for i in range(2)]
                p1bf = [Em(f"p1bf{i}", [128, C], BF16) for i in range(2)]
                p2stg = [Em(f"p2stg{i}", [128, HBS * 128], F32) for i in range(2)]
                p2bf = [Em(f"p2bf{i}", [128, HBS * 128], BF16) for i in range(2)]
                sq2 = [Em(f"sq2{i}", [128, TH], BF16) for i in range(2)]
                mb2 = Em("mb2", [128, TH], F32)
                t22 = Em("t22", [128, TH], F32)
                t32 = Em("t32", [128, TH], BF16)
                ab2 = Em("ab2", [128, TH], BF16)
                bma2 = Em("bma2", [128, TH], BF16)
                outsb = [mb2, t22]

                pf_p1 = [0, 0]
                for hf in range(2):
                    if hf == 1:
                        pf_p1 = [p.cnt["ld_p10"], p.cnt["ld_p11"]]

                        def phE_prefetch(sync):
                            p.w(sync, "s_pool", p.cnt["s_pool"])
                            for hc in range(2):
                                p.inc(sync.dma_start(
                                    out=p1stg[hc % 2][:]
                                    .rearrange("p (c i) -> p c i", i=128),
                                    in_=p1T[:, hc * 128:(hc + 1) * 128]
                                    .rearrange("(c p) i -> p c i", p=128)),
                                    f"ld_p1{hc % 2}", 16)
                        block.sync(phE_prefetch)
                    full_barrier(skip_sync=True)
                    tofs = hf * TH
                    bE = dict(p.cnt)

                    if hf == 1:
                        def phE0_sync(sync, bE=bE):
                            p.w(sync, "s_dve", bE["s_dve"])
                            for sl in range(2):
                                p.w(sync, f"st_r1{sl}", 16 * (NCC // 2))
                            p.inc(sync.dma_start(
                                out=r1acc[:].rearrange("p (c t) -> p c t", t=TH),
                                in_=r1T.rearrange("(c p) t -> p c t", p=128)),
                                "ld_r1", 16)
                        block.sync(phE0_sync)

                    with ExitStack() as esE1:
                        psS2 = esE1.enter_context(
                            nc.psum_tensor(f"psS2_{hf}", [128, TH], F32))
                        psSS2 = esE1.enter_context(
                            nc.psum_tensor(f"psSS2_{hf}", [128, TH], F32))

                        def phE1_cast(v, hf=hf, bE=bE):
                            for c in range(NCC):
                                if hf == 1:
                                    p.w(v, "ld_r1", 16)
                                cs = slice(c * TH, (c + 1) * TH)
                                p.inc(v.tensor_copy(xn2[:, cs], r1acc[:, cs]),
                                      "s_pool")
                        block.vector(phE1_cast)

                        def phE1_sq(a, hf=hf, bE=bE):
                            for c in range(NCC):
                                if hf == 1:
                                    p.w(a, "ld_r1", 16)
                                p.w(a, "s_pe", bE["s_pe"] + c - 1)
                                cs = slice(c * TH, (c + 1) * TH)
                                p.inc(a.activation(sq2[c % 2][:], r1acc[:, cs],
                                                   AF.Square), "s_act")
                        block.scalar(phE1_sq)

                        def phE1_pe(t, bE=bE):
                            p.w(t, "s_dve", bE["s_dve"])
                            p.w(t, "s_act", bE["s_act"])
                            for c in range(NCC):
                                p.w(t, "s_pool", bE["s_pool"] + c + 1)
                                p.w(t, "s_act", bE["s_act"] + c + 1)
                                for ns in range(NSH):
                                    sl = slice(ns * SWT, (ns + 1) * SWT)
                                    t.matmul(psS2[:, sl], ones128[:],
                                             xn2[:, c * TH + ns * SWT:
                                                 c * TH + (ns + 1) * SWT],
                                             start=(c == 0), stop=(c == NCC - 1))
                                    ins = t.matmul(psSS2[:, sl], ones128[:],
                                                   sq2[c % 2][:, sl],
                                                   start=(c == 0),
                                                   stop=(c == NCC - 1))
                                p.inc(ins, "s_pe")
                        block.tensor(phE1_pe)

                        def phE2_v1(v, bE=bE):
                            p.w(v, "s_pe", bE["s_pe"] + NCC)
                            p.inc(v.tensor_scalar(out=mb2[:], in0=psS2[:],
                                                  scalar1=1.0 / C, scalar2=None,
                                                  op0=ALU.mult), "s_dve")
                        block.vector(phE2_v1)

                        def phE2_s1(s):
                            p.w(s, "s_dve", p.cnt["s_dve"])
                            p.inc(s.activation(t22[:], mb2[:], AF.Square), "s_act")
                        block.scalar(phE2_s1)

                        def phE2_v2(v):
                            p.w(v, "s_act", p.cnt["s_act"])
                            p.inc(v.scalar_tensor_tensor(
                                out=t22[:], in0=psSS2[:], scalar=1.0 / C,
                                in1=t22[:], op0=ALU.mult, op1=ALU.subtract),
                                "s_dve")
                        block.vector(phE2_v2)

                        def phE2_s2(s):
                            p.w(s, "s_dve", p.cnt["s_dve"])
                            p.inc(s.activation(t22[:], t22[:], AF.Ln,
                                               bias=f32(EPS)), "s_act")
                            p.w(s, "s_act", p.cnt["s_act"])
                            p.inc(s.activation(ab2[:], t22[:], AF.Exp,
                                               scale=f32(-0.5)), "s_act")
                        block.scalar(phE2_s2)

                        def phE2_v3(v):
                            p.w(v, "s_act", p.cnt["s_act"])
                            p.inc(v.scalar_tensor_tensor(
                                out=bma2[:], in0=mb2[:], scalar=-1.0, in1=ab2[:],
                                op0=ALU.mult, op1=ALU.mult), "s_dve")
                            for c in range(NCC):
                                cs = slice(c * TH, (c + 1) * TH)
                                p.w(v, "s_dve", p.cnt["s_dve"])
                                p.inc(v.tensor_mul(t32[:], xn2[:, cs], ab2[:]),
                                      "s_dve")
                                p.w(v, "s_dve", p.cnt["s_dve"])
                                p.inc(v.tensor_add(xn2[:, cs], t32[:], bma2[:]),
                                      "s_dve")
                        block.vector(phE2_v3)

                    full_barrier(skip_sync=True)
                    with ExitStack() as esE4:
                        psh = [esE4.enter_context(
                            nc.psum_tensor(f"psh{hf}_{i}", [128, TH], F32))
                            for i in range(2)]
                        pso2 = [esE4.enter_context(
                            nc.psum_tensor(f"pso2{hf}_{i}", [128, TH], F32))
                            for i in range(2)]
                        bE4 = dict(p.cnt)
                        stob = [p.cnt["st_o0"], p.cnt["st_o1"]]
                        b2f_ap = lambda i: b2f[:, i:i + 1]
                        ldp1 = (pf_p1 if hf == 1
                                else [p.cnt["ld_p10"], p.cnt["ld_p11"]])
                        ldp2 = [p.cnt["ld_p20"], p.cnt["ld_p21"]]

                        def phE4_sync(sync, bE4=bE4, hf=hf):
                            for hb in range(NHB):
                                for hc in range(HBS):
                                    if hf == 1 and hb == 0 and hc < 2:
                                        continue
                                    if hf == 0 and hb == 0 and hc < 2:
                                        p.w(sync, "s_pe_pv", NKV)
                                    else:
                                        p.w(sync, "s_pool",
                                            bE4["s_pool"] + hb * (HBS + NCC)
                                            + hc - 1)
                                    hcg = hb * HBS + hc
                                    p.inc(sync.dma_start(
                                        out=p1stg[hc % 2][:]
                                        .rearrange("p (c i) -> p c i", i=128),
                                        in_=p1T[:, hcg * 128:(hcg + 1) * 128]
                                        .rearrange("(c p) i -> p c i", p=128)),
                                        f"ld_p1{hc % 2}", 16)
                                for i in range(NCC):
                                    if hf == 0 and hb == 0 and i < 2:
                                        p.w(sync, "s_pe_pv", NKV)
                                    else:
                                        p.w(sync, "s_pool",
                                            bE4["s_pool"] + hb * (HBS + NCC)
                                            + HBS + i - 1)
                                    p.inc(sync.dma_start(
                                        out=p2stg[i % 2][:]
                                        .rearrange("p (c j) -> p c j", j=128),
                                        in_=p2T[hb * HBS * 128:
                                                (hb + 1) * HBS * 128,
                                                i * 128:(i + 1) * 128]
                                        .rearrange("(c p) j -> p c j", p=128)),
                                        f"ld_p2{i % 2}", 16)
                        block.sync(phE4_sync)

                        def phE4_cast(v, bE4=bE4, ldp1=ldp1, ldp2=ldp2):
                            pend = []
                            for hb in range(NHB):
                                for hc in range(HBS):
                                    nf = hb * (HBS // 2) + hc // 2 + 1
                                    p.w(v, f"ld_p1{hc % 2}",
                                        ldp1[hc % 2] + 16 * nf)
                                    p.w(v, "s_pe",
                                        bE4["s_pe"] + hb * (HBS + NCC) + hc - 1)
                                    p.inc(v.tensor_copy(p1bf[hc % 2][:],
                                                        p1stg[hc % 2][:]),
                                          "s_pool")
                                    if hc < len(pend):
                                        pend[hc]()
                                pend = []

                                def evac2(i, hb=hb):
                                    p.w(v, "s_pe",
                                        bE4["s_pe"] + hb * (HBS + NCC)
                                        + HBS + i + 1)
                                    p.w(v, "s_dve",
                                        bE4["s_dve"] + hb * NCC + i)
                                    cs = slice(i * TH, (i + 1) * TH)
                                    if hb == NHB - 1:
                                        p.w(v, f"st_o{i % 2}",
                                            stob[i % 2] + 16 * (i // 2))
                                        p.inc(v.scalar_tensor_tensor(
                                            out=outsb[i % 2][:],
                                            in0=pso2[i % 2][:],
                                            scalar=b2f_ap(i),
                                            in1=r1acc[:, cs],
                                            op0=ALU.add, op1=ALU.add), "s_dve")
                                    else:
                                        p.inc(v.tensor_add(r1acc[:, cs],
                                                           pso2[i % 2][:],
                                                           r1acc[:, cs]), "s_dve")
                                for i in range(NCC):
                                    nf = hb * (NCC // 2) + i // 2 + 1
                                    p.w(v, f"ld_p2{i % 2}",
                                        ldp2[i % 2] + 16 * nf)
                                    p.w(v, "s_pe",
                                        bE4["s_pe"] + hb * (HBS + NCC)
                                        + HBS + i - 1)
                                    p.inc(v.tensor_copy(p2bf[i % 2][:],
                                                        p2stg[i % 2][:]),
                                          "s_pool")
                                    if i >= 2:
                                        evac2(i - 2)
                                if hb == NHB - 1:
                                    evac2(NCC - 2)
                                    evac2(NCC - 1)
                                else:
                                    pend = [lambda i=NCC - 2, hb=hb: evac2(i, hb),
                                            lambda i=NCC - 1, hb=hb: evac2(i, hb)]
                        block.vector(phE4_cast)

                        def phE4_pe(t, bE4=bE4):
                            for hb in range(NHB):
                                for hc in range(HBS):
                                    p.w(t, "s_pool",
                                        bE4["s_pool"] + hb * (HBS + NCC) + hc + 1)
                                    p.w(t, "s_act",
                                        bE4["s_act"] + hb * HBS + hc - 1)
                                    for c in range(NCC):
                                        for ns in range(NSH):
                                            sl = slice(ns * SWT, (ns + 1) * SWT)
                                            ins = t.matmul(
                                                psh[hc % 2][:, sl],
                                                p1bf[hc % 2][:,
                                                             c * 128:(c + 1) * 128],
                                                xn2[:, c * TH + ns * SWT:
                                                    c * TH + (ns + 1) * SWT],
                                                start=(c == 0),
                                                stop=(c == NCC - 1))
                                    p.inc(ins, "s_pe")
                                for i in range(NCC):
                                    p.w(t, "s_pool",
                                        bE4["s_pool"] + hb * (HBS + NCC)
                                        + HBS + i + 1)
                                    p.w(t, "s_act",
                                        bE4["s_act"] + (hb + 1) * HBS)
                                    p.w(t, "s_dve",
                                        bE4["s_dve"] + hb * NCC + i - 1)
                                    for hc in range(HBS):
                                        for ns in range(NSH):
                                            sl = slice(ns * SWT, (ns + 1) * SWT)
                                            ins = t.matmul(
                                                pso2[i % 2][:, sl],
                                                p2bf[i % 2][:,
                                                            hc * 128:(hc + 1) * 128],
                                                hblk[:, hc * TH + ns * SWT:
                                                     hc * TH + (ns + 1) * SWT],
                                                start=(hc == 0),
                                                stop=(hc == HBS - 1))
                                    p.inc(ins, "s_pe")
                        block.tensor(phE4_pe)

                        def phE4_act(s, bE4=bE4):
                            for hb in range(NHB):
                                for hc in range(HBS):
                                    p.w(s, "s_pe",
                                        bE4["s_pe"] + hb * (HBS + NCC) + hc + 1)
                                    p.inc(s.activation(
                                        hblk[:, hc * TH:(hc + 1) * TH],
                                        psh[hc % 2][:], AF.Gelu,
                                        bias=b1f[:, hb * HBS + hc:
                                                 hb * HBS + hc + 1]), "s_act")
                        block.scalar(phE4_act)


                        def phE5_st(g, bE4=bE4, stob=stob, tofs=tofs):
                            for i in range(NCC):
                                p.w(g, "s_dve",
                                    bE4["s_dve"] + (NHB - 1) * NCC + i + 1)
                                p.inc(g.dma_start(
                                    out=outT[i * 128:(i + 1) * 128,
                                             tofs:tofs + TH],
                                    in_=outsb[i % 2][:]), f"st_o{i % 2}", 16)
                        block.gpsimd(phE5_st)

                def fin(sync):
                    p.w(sync, "st_o0", p.cnt["st_o0"])
                    p.w(sync, "st_o1", p.cnt["st_o1"])
                block.sync(fin)
        esBIG.close()
    return nc


def _host_tables(H):
    D = 128
    inv = 1.0 / 10000.0 ** (np.arange(0, D, 2, dtype=np.float64) / D)
    hidx = np.arange(H, dtype=np.float64)
    freqs = np.outer(hidx, inv)
    emb = np.concatenate([freqs, freqs], axis=1)
    cos_t = np.ascontiguousarray(np.cos(emb).T.astype(np.float32))
    sin = np.sin(emb).T.astype(np.float32)
    ssin = sin.copy()
    ssin[:64] = -sin[:64]
    ssin = np.ascontiguousarray(ssin)
    tril = np.tril(np.ones((128, 128), dtype=np.float32))
    etril = np.where(tril > 0, np.float32(np.e),
                     np.float32(1.0)).astype(np.float32)
    ident = np.eye(128, dtype=np.float32)
    return cos_t, ssin, etril, ident


def kernel(L0_attn_x, L0_attn_wq, L0_attn_bq, L0_attn_wk, L0_attn_bk,
           L0_attn_wv, L0_attn_bv, L0_attn_wo, L0_attn_bo,
           L0_ffn_proj1, L0_ffn_proj1_bias, L0_ffn_proj2, L0_ffn_proj2_bias,
           L0_ffn_gate_w):
    x = np.asarray(L0_attn_x)
    Bn, T, C = x.shape
    H = 16
    KVH = 4
    HF = np.asarray(L0_ffn_proj1).shape[0]
    KVD = KVH * 128
    assert C == H * 128

    cos_t, ssin_t, etril, ident = _host_tables(H)

    def fm(b):
        return np.ascontiguousarray(
            np.asarray(b).reshape(-1, 128).T.astype(np.float32))

    common = {
        "wqT": np.ascontiguousarray(np.asarray(L0_attn_wq).T),
        "wkT": np.ascontiguousarray(np.asarray(L0_attn_wk).T),
        "wvT": np.ascontiguousarray(np.asarray(L0_attn_wv).T),
        "woT": np.ascontiguousarray(np.asarray(L0_attn_wo).T),
        "p1T": np.ascontiguousarray(np.asarray(L0_ffn_proj1).T),
        "p2T": np.ascontiguousarray(np.asarray(L0_ffn_proj2).T),
        "bq_fm": fm(L0_attn_bq),
        "bk_fm": fm(L0_attn_bk),
        "bv_row": np.ascontiguousarray(
            np.asarray(L0_attn_bv).reshape(1, KVD).astype(np.float32)),
        "bo_fm": fm(L0_attn_bo),
        "b1_fm": fm(L0_ffn_proj1_bias),
        "b2_fm": fm(L0_ffn_proj2_bias),
        "cos_t": cos_t, "ssin_t": ssin_t, "etril": etril, "ident": ident,
        "gate_w": np.ascontiguousarray(
            np.asarray(L0_ffn_gate_w).reshape(-1, 128, 8)
            .transpose(1, 0, 2).reshape(128, -1).astype(np.float32)),
    }
    in_maps = []
    for b in range(Bn):
        m = dict(common)
        m["xT"] = np.ascontiguousarray(x[b].T)
        in_maps.append(m)

    nc = build_nc(T=T, H=H, KVH=KVH, HF=HF)
    res = run_bass_kernel_spmd(nc, in_maps, core_ids=list(range(Bn)))
    out = np.empty((Bn, T, C), dtype=np.float32)
    for b in range(Bn):
        out[b] = res.results[b]["outT"].T
    return out


# version 16
# speedup vs baseline: 1.0027x; 1.0027x over previous
import sys
sys.path.insert(0, "/opt/trn_rl_repo")
from contextlib import ExitStack
import numpy as np
import ml_dtypes
import concourse.bass as bass
import concourse.mybir as mybir
from concourse.bass_utils import run_bass_kernel_spmd

F32 = mybir.dt.float32
BF16 = mybir.dt.bfloat16
AF = mybir.ActivationFunctionType
ALU = mybir.AluOpType
AX = mybir.AxisListType
f32 = np.float32

EPS = 1e-5
MB = 20.0


class P:
    def __init__(self, nc):
        self.nc = nc
        self.sem = {}
        self.cnt = {}

    def new_sem(self, es, name):
        self.sem[name] = es.enter_context(self.nc.semaphore(name))
        self.cnt[name] = 0

    def inc(self, ins, name, n=1):
        self.cnt[name] += n
        ins.then_inc(self.sem[name], n)
        return ins

    def w(self, eng, name, val):
        if val > 0:
            eng.wait_ge(self.sem[name], val)


def build_nc(T=2048, H=16, KVH=4, HF=8192, stop_after=None):
    D = 128
    C = H * D
    G = H // KVH
    NCC = C // 128
    NT = T // 128
    NHC = HF // 128
    HBS = min(16, NHC)
    NHB = NHC // HBS
    TH = T // 2
    SW = min(512, T)
    SWH = min(512, T // 2)
    SWT = min(512, TH)
    NS = T // SW
    NS2 = (T // 2) // SWH
    NSH = TH // SWT
    KVD = KVH * D
    NU = NT * H
    NKV = NT * KVH
    assert T % 256 == 0 and H % KVH == 0 and NHC % HBS == 0 and NCC == H

    nc = bass.Bass("TRN2", target_bir_lowering=False)

    xT = nc.dram_tensor("xT", [C, T], F32, kind="ExternalInput").ap()
    wqT = nc.dram_tensor("wqT", [C, C], F32, kind="ExternalInput").ap()
    wkT = nc.dram_tensor("wkT", [C, KVD], F32, kind="ExternalInput").ap()
    wvT = nc.dram_tensor("wvT", [C, KVD], F32, kind="ExternalInput").ap()
    woT = nc.dram_tensor("woT", [C, C], F32, kind="ExternalInput").ap()
    p1T = nc.dram_tensor("p1T", [C, HF], F32, kind="ExternalInput").ap()
    p2T = nc.dram_tensor("p2T", [HF, C], F32, kind="ExternalInput").ap()
    bq_h = nc.dram_tensor("bq_fm", [128, H], F32, kind="ExternalInput").ap()
    bk_h = nc.dram_tensor("bk_fm", [128, KVH], F32, kind="ExternalInput").ap()
    bv_h = nc.dram_tensor("bv_row", [1, KVD], F32, kind="ExternalInput").ap()
    bo_h = nc.dram_tensor("bo_fm", [128, NCC], F32, kind="ExternalInput").ap()
    b1_h = nc.dram_tensor("b1_fm", [128, NHC], F32, kind="ExternalInput").ap()
    b2_h = nc.dram_tensor("b2_fm", [128, NCC], F32, kind="ExternalInput").ap()
    cos_h = nc.dram_tensor("cos_t", [128, H], F32, kind="ExternalInput").ap()
    ssin_h = nc.dram_tensor("ssin_t", [128, H], F32, kind="ExternalInput").ap()
    etril_h = nc.dram_tensor("etril", [128, 128], F32, kind="ExternalInput").ap()
    ident_h = nc.dram_tensor("ident", [128, 128], F32, kind="ExternalInput").ap()
    gate_h = nc.dram_tensor("gate_w", [128, (C // 128) * 8], F32,
                            kind="ExternalInput").ap()
    r1T = nc.dram_tensor("r1T", [C, T // 2], F32, kind="Internal").ap()
    outT = nc.dram_tensor("outT", [C, T], F32, kind="ExternalOutput").ap()

    p = P(nc)
    es0 = ExitStack()
    with es0:
        for s in ["ld_c", "ld_x0", "ld_x1", "ld_xg", "ld_w0", "ld_w1",
                  "ld_p10", "ld_p11", "ld_p20", "ld_p21", "ld_r1",
                  "st_r10", "st_r11", "st_o0", "st_o1",
                  "s_pool", "s_pe", "s_act", "s_dve",
                  "s_pe_sc", "s_pe_tr", "s_pe_pv",
                  "s_act_exp", "s_act_af", "s_dve_c", "s_dve_p", "s_dve_ev"]:
            p.new_sem(es0, s)

        SB = lambda n, shp, dt: es0.enter_context(nc.sbuf_tensor(n, shp, dt))

        etril = SB("etril_s", [128, 128], F32)
        ident = SB("ident_s", [128, 128], BF16)
        identf = SB("identf_s", [128, 128], F32)
        cos_s = SB("cos_s", [128, H], F32)
        ssin_s = SB("ssin_s", [128, H], F32)
        bqf = SB("bqf", [128, H], F32)
        bkf = SB("bkf", [128, KVH], F32)
        bvr = SB("bvr", [1, KVD], F32)
        bvr_bf = SB("bvr_bf", [1, KVD], BF16)
        ones1 = SB("ones1", [1, 128], BF16)
        ones128 = SB("ones128", [128, 128], BF16)
        bof = SB("bof", [128, NCC], F32)
        b1f = SB("b1f", [128, NHC], F32)
        b2f = SB("b2f", [128, NCC], F32)
        bqcos = SB("bqcos", [128, H], F32)
        bqssin = SB("bqssin", [128, H], F32)
        bkcos = SB("bkcos", [128, KVH], F32)
        bkssin = SB("bkssin", [128, KVH], F32)
        shq = SB("shq", [128, H], F32)
        shk = SB("shk", [128, KVH], F32)
        gate_s = SB("gate_s", [128, (C // 128) * 8], F32)
        Zp = [SB(f"Zp{i}", [128, 8], F32) for i in range(2)]
        Zs = [SB(f"Zs{i}", [128, 1], F32) for i in range(2)]
        rz = [SB(f"rz{i}", [128, 1], F32) for i in range(2)]

        esBIG = ExitStack()
        qbuf = esBIG.enter_context(nc.sbuf_tensor("qbuf", [128, H * T], BF16))
        xn1 = esBIG.enter_context(nc.sbuf_tensor("xn1", [128, NCC * T], BF16))
        xb = qbuf
        attfm = qbuf
        p_sb = [lambda i=i: xn1[:, i * T:(i + 1) * T] for i in range(3)]
        pTg_of = lambda sl, g: xn1[:, (3 + sl * G + g) * T:(4 + sl * G + g) * T]

        esB2 = ExitStack()

        with nc.Block() as block:
            def full_barrier(skip_sync=False):
                tgt = {k: v for k, v in p.cnt.items() if v > 0}
                regs = [block.gpsimd, block.tensor, block.scalar, block.vector]
                if not skip_sync:
                    regs.append(block.sync)
                for reg in regs:
                    def _bar(eng, tgt=tgt):
                        for k, v in tgt.items():
                            eng.wait_ge(p.sem[k], v)
                    reg(_bar)

            def ph0_sync(sync):
                for dst, src in [(etril[:], etril_h), (identf[:], ident_h),
                                 (gate_s[:], gate_h),
                                 (cos_s[:], cos_h), (ssin_s[:], ssin_h),
                                 (bqf[:], bq_h), (bkf[:], bk_h), (bvr[:], bv_h),
                                 (bof[:], bo_h), (b1f[:], b1_h), (b2f[:], b2_h)]:
                    p.inc(sync.dma_start(out=dst, in_=src), "ld_c", 16)
            block.sync(ph0_sync)

            def ph0_pool(g):
                p.w(g, "ld_c", p.cnt["ld_c"])
                g.memset(ones1[:], 1.0)
                g.memset(ones128[:], 1.0)
                g.tensor_copy(ident[:], identf[:])
                p.inc(g.tensor_copy(bvr_bf[:], bvr[:]), "s_pool")
            block.gpsimd(ph0_pool)

            def ph0_dve(v):
                p.w(v, "ld_c", p.cnt["ld_c"])
                v.tensor_mul(bqcos[:], bqf[:], cos_s[:])
                v.tensor_copy(shq[0:64, :], bqf[64:128, :])
                p.inc(v.tensor_copy(shq[64:128, :], bqf[0:64, :]), "s_dve")
                p.w(v, "s_dve", p.cnt["s_dve"])
                v.tensor_mul(bqssin[:], shq[:], ssin_s[:])
                v.tensor_mul(bkcos[:], bkf[:], cos_s[:, 0:KVH])
                v.tensor_copy(shk[0:64, :], bkf[64:128, :])
                p.inc(v.tensor_copy(shk[64:128, :], bkf[0:64, :]), "s_dve")
                p.w(v, "s_dve", p.cnt["s_dve"])
                p.inc(v.tensor_mul(bkssin[:], shk[:], ssin_s[:, 0:KVH]), "s_dve")
            block.vector(ph0_dve)

            esW = ExitStack()
            wstg = [esW.enter_context(nc.sbuf_tensor(f"wstg{i}", [128, C], F32))
                    for i in range(2)]
            wbf = [esW.enter_context(nc.sbuf_tensor(f"wbf{i}", [128, C], BF16))
                   for i in range(2)]

            full_barrier()
            with ExitStack() as esA:
                A = lambda n, shp, dt: esA.enter_context(nc.sbuf_tensor(n, shp, dt))
                xstg = [A(f"xstg{i}", [128, T], F32) for i in range(2)]
                sq = [A(f"sq{i}", [128, T], BF16) for i in range(2)]
                mb_b = A("mb_b", [128, T], F32)
                t2_b = A("t2_b", [128, T], F32)
                ab_b = A("ab_b", [128, T], BF16)
                bma_b = A("bma_b", [128, T], BF16)
                psS = esA.enter_context(nc.psum_tensor("psS", [128, T], F32))
                psSS = esA.enter_context(nc.psum_tensor("psSS", [128, T], F32))
                bA = dict(p.cnt)

                def phA_sync(sync):
                    for c in range(0, NCC, 2):
                        p.w(sync, "s_pool", bA["s_pool"] + c - 1)
                        p.w(sync, "s_act", bA["s_act"] + c - 1)
                        p.inc(sync.dma_start(out=xstg[0][:],
                                             in_=xT[c * 128:(c + 1) * 128, :]),
                              "ld_x0", 16)
                block.sync(phA_sync)

                def phA_syncb(g):
                    for c in range(1, NCC, 2):
                        p.w(g, "s_pool", bA["s_pool"] + c - 1)
                        p.w(g, "s_act", bA["s_act"] + c - 1)
                        p.inc(g.dma_start(out=xstg[1][:],
                                          in_=xT[c * 128:(c + 1) * 128, :]),
                              "ld_xg", 16)
                block.gpsimd(phA_syncb)

                def phA_v0(v):
                    for c in range(NCC):
                        p.w(v, "ld_x0" if c % 2 == 0 else "ld_xg",
                            16 * (c // 2 + 1))
                        p.inc(v.tensor_copy(xb[:, c * T:(c + 1) * T],
                                            xstg[c % 2][:]), "s_pool")
                block.vector(phA_v0)

                def phA_sq(a):
                    for c in range(NCC):
                        p.w(a, "ld_x0" if c % 2 == 0 else "ld_xg",
                            16 * (c // 2 + 1))
                        p.w(a, "s_pe", bA["s_pe"] + c - 1)
                        p.inc(a.activation(sq[c % 2][:], xstg[c % 2][:],
                                           AF.Square), "s_act")
                block.scalar(phA_sq)

                def phA_pe(t):
                    for c in range(NCC):
                        p.w(t, "s_pool", bA["s_pool"] + c + 1)
                        p.w(t, "s_act", bA["s_act"] + c + 1)
                        for ns in range(NS):
                            sl = slice(ns * SW, (ns + 1) * SW)
                            t.matmul(psS[:, sl], ones128[:],
                                     xb[:, c * T + ns * SW:c * T + (ns + 1) * SW],
                                     start=(c == 0), stop=(c == NCC - 1))
                            ins = t.matmul(psSS[:, sl], ones128[:], sq[c % 2][:, sl],
                                           start=(c == 0), stop=(c == NCC - 1))
                        p.inc(ins, "s_pe")
                block.tensor(phA_pe)

                def phA_v1(v):
                    p.w(v, "s_pe", bA["s_pe"] + NCC)
                    p.inc(v.tensor_scalar(out=mb_b[:], in0=psS[:], scalar1=1.0 / C,
                                          scalar2=None, op0=ALU.mult), "s_dve")
                block.vector(phA_v1)

                def phA_s1(s):
                    p.w(s, "s_dve", p.cnt["s_dve"])
                    p.inc(s.activation(t2_b[:], mb_b[:], AF.Square), "s_act")
                block.scalar(phA_s1)

                def phA_v2(v):
                    p.w(v, "s_act", p.cnt["s_act"])
                    p.inc(v.scalar_tensor_tensor(out=t2_b[:], in0=psSS[:],
                                                 scalar=1.0 / C, in1=t2_b[:],
                                                 op0=ALU.mult, op1=ALU.subtract),
                          "s_dve")
                block.vector(phA_v2)

                def phA_s2(s):
                    p.w(s, "s_dve", p.cnt["s_dve"])
                    p.inc(s.activation(t2_b[:], t2_b[:], AF.Ln, bias=f32(EPS)),
                          "s_act")
                    p.w(s, "s_act", p.cnt["s_act"])
                    p.inc(s.activation(ab_b[:], t2_b[:], AF.Exp, scale=f32(-0.5)),
                          "s_act")
                block.scalar(phA_s2)

                def phA_v3(v):
                    p.w(v, "s_act", p.cnt["s_act"])
                    p.inc(v.scalar_tensor_tensor(out=bma_b[:], in0=mb_b[:],
                                                 scalar=-1.0, in1=ab_b[:],
                                                 op0=ALU.mult, op1=ALU.mult),
                          "s_dve")
                    for c in range(NCC):
                        cs = slice(c * T, (c + 1) * T)
                        p.w(v, "s_dve", p.cnt["s_dve"])
                        p.inc(v.tensor_mul(xn1[:, cs], xb[:, cs], ab_b[:]),
                              "s_dve")
                        p.w(v, "s_dve", p.cnt["s_dve"])
                        p.inc(v.tensor_add(xn1[:, cs], xn1[:, cs], bma_b[:]),
                              "s_dve")
                block.vector(phA_v3)

            b1ev = [0]
            kbuf = esB2.enter_context(
                nc.sbuf_tensor("kbuf", [128, KVH * T], BF16))
            vTb = esB2.enter_context(
                nc.sbuf_tensor("vTb", [128, NT * KVD], BF16))
            full_barrier(skip_sync=True)
            with ExitStack() as esB:
                Bm = lambda n, shp, dt: esB.enter_context(nc.sbuf_tensor(n, shp, dt))

                with ExitStack() as esB0:
                    wvbf = esB0.enter_context(
                        nc.sbuf_tensor("wvbf", [128, NCC * KVD], BF16))
                    psvt = [esB0.enter_context(
                        nc.psum_tensor(f"psvt{i}", [128, KVD], F32))
                        for i in range(2)]
                    bB0 = dict(p.cnt)
                    CPS = (128 * C) // (128 * KVD)
                    NQ = NCC // CPS

                    def phB0_sync(sync):
                        for q in range(NQ):
                            p.w(sync, "s_pool", bB0["s_pool"] + q - 1)
                            p.inc(sync.dma_start(
                                out=wstg[q % 2][:, 0:CPS * KVD]
                                .rearrange("p (c i) -> p c i", i=KVD),
                                in_=wvT[q * CPS * 128:(q + 1) * CPS * 128, :]
                                .rearrange("(c p) i -> p c i", p=128)),
                                f"ld_w{q % 2}", 16)
                    block.sync(phB0_sync)

                    def phB0_cast(v):
                        for q in range(NQ):
                            p.w(v, f"ld_w{q % 2}", 16 * (q // 2 + 1))
                            p.inc(v.tensor_copy(
                                wvbf[:, q * CPS * KVD:(q + 1) * CPS * KVD],
                                wstg[q % 2][:, 0:CPS * KVD]), "s_pool")
                    block.vector(phB0_cast)

                    def phB0_pe(t):
                        p.w(t, "s_dve", bB0["s_dve"])
                        p.w(t, "s_pool", bB0["s_pool"] + NQ)
                        for j in range(NT):
                            p.w(t, "s_act", bB0["s_act"] + j - 1)
                            for c in range(NCC):
                                t.matmul(psvt[j % 2][:],
                                         xn1[:, c * T + j * 128:c * T + (j + 1) * 128],
                                         wvbf[:, c * KVD:(c + 1) * KVD],
                                         start=(c == 0), stop=False)
                            p.inc(t.matmul(psvt[j % 2][:], ones1[:], bvr_bf[:],
                                           start=False, stop=True), "s_pe")
                    block.tensor(phB0_pe)

                    def phB0_act(s):
                        for j in range(NT):
                            p.w(s, "s_pe", bB0["s_pe"] + j + 1)
                            p.inc(s.activation(vTb[:, j * KVD:(j + 1) * KVD],
                                               psvt[j % 2][:], AF.Identity,
                                               bias=f32(0.0)), "s_act")
                    block.scalar(phB0_act)

                full_barrier(skip_sync=True)
                with ExitStack() as esB1:
                    B1 = lambda n, shp, dt: esB1.enter_context(
                        nc.sbuf_tensor(n, shp, dt))
                    qc_b = B1("qc_b", [128, T], F32)
                    tmp_b = B1("tmp_b", [128, T], F32)
                    psqk = [esB1.enter_context(
                        nc.psum_tensor(f"psqk{i}", [128, T], F32))
                        for i in range(2)]
                    bB = dict(p.cnt)
                    ldw_b = [p.cnt["ld_w0"], p.cnt["ld_w1"]]
                    NOC = KVH + H

                    def phB_sync(sync):
                        for oc in range(NOC):
                            src, idx = (wkT, oc) if oc < KVH else (wqT, oc - KVH)
                            p.w(sync, "s_pool", bB["s_pool"] + oc - 1)
                            p.inc(sync.dma_start(
                                out=wstg[oc % 2][:]
                                .rearrange("p (c i) -> p c i", i=128),
                                in_=src[:, idx * 128:(idx + 1) * 128]
                                .rearrange("(c p) i -> p c i", p=128)),
                                f"ld_w{oc % 2}", 16)
                    block.sync(phB_sync)


                    b1ev[0] = bB["s_dve"]

                    def phB_pe(t):
                        p.w(t, "s_act", bB["s_act"])
                        for oc in range(NOC):
                            p.w(t, "s_pool", bB["s_pool"] + oc + 1)
                            p.w(t, "s_act", bB["s_act"] + oc - 1)
                            for c in range(NCC):
                                for ns in range(NS):
                                    ins = t.matmul(
                                        psqk[oc % 2][:, ns * SW:(ns + 1) * SW],
                                        wbf[oc % 2][:, c * 128:(c + 1) * 128],
                                        xn1[:, c * T + ns * SW:
                                            c * T + (ns + 1) * SW],
                                        start=(c == 0), stop=(c == NCC - 1))
                            p.inc(ins, "s_pe")
                    block.tensor(phB_pe)

                    def phB_act(s):
                        for oc in range(NOC):
                            isk = oc < KVH
                            idx = oc if isk else oc - KVH
                            ct = bkcos if isk else bqcos
                            st_ = bkssin if isk else bqssin
                            p.w(s, "s_pe", bB["s_pe"] + oc + 1)
                            p.w(s, "s_dve", bB["s_dve"] + oc)
                            s.activation(qc_b[:], psqk[oc % 2][:], AF.Identity,
                                         bias=ct[:, idx:idx + 1],
                                         scale=cos_s[:, idx:idx + 1])
                            s.activation(tmp_b[0:64, :], psqk[oc % 2][64:128, :],
                                         AF.Identity,
                                         bias=st_[0:64, idx:idx + 1],
                                         scale=ssin_s[0:64, idx:idx + 1])
                            p.inc(s.activation(tmp_b[64:128, :],
                                               psqk[oc % 2][0:64, :], AF.Identity,
                                               bias=st_[64:128, idx:idx + 1],
                                               scale=ssin_s[64:128, idx:idx + 1]),
                                  "s_act")
                    block.scalar(phB_act)

                    def phB_dve(v):
                        def evac(oc):
                            isk = oc < KVH
                            idx = oc if isk else oc - KVH
                            dst = (kbuf[:, idx * T:(idx + 1) * T] if isk
                                   else qbuf[:, idx * T:(idx + 1) * T])
                            p.w(v, "s_act", bB["s_act"] + oc + 1)
                            p.inc(v.tensor_add(dst, qc_b[:], tmp_b[:]), "s_dve")
                        for oc in range(NOC):
                            p.w(v, f"ld_w{oc % 2}",
                                ldw_b[oc % 2] + 16 * (oc // 2 + 1))
                            p.w(v, "s_pe", bB["s_pe"] + oc - 1)
                            p.inc(v.tensor_copy(wbf[oc % 2][:], wstg[oc % 2][:]),
                                  "s_pool")
                            if oc >= 1:
                                evac(oc - 1)
                        evac(NOC - 1)
                    block.vector(phB_dve)

            if stop_after == "B":
                esB2.close(); esBIG.close()
                return nc
            full_barrier(skip_sync=True)
            with ExitStack() as esC:
                scF = esC.enter_context(nc.psum_tensor("scF", [128, T], F32))
                ps_t = esC.enter_context(nc.psum_tensor("ps_t", [128, T], BF16))
                pvps = [esC.enter_context(
                    nc.psum_tensor(f"pvps{i}", [128, G * 128], F32))
                    for i in range(2)]
                bC = dict(p.cnt)

                def ranges_of(tt):
                    b = tt * 128
                    r = []
                    if b > 0:
                        r.append((0, b, -MB + 1.0))
                    r.append((b, T, -MB))
                    return r

                def n_accum_cols(tt):
                    n = 0
                    for sh in range(2):
                        h0, h1 = sh * (T // 2), (sh + 1) * (T // 2)
                        for (lo, hi, _bias) in ranges_of(tt):
                            if max(lo, h0) < min(hi, h1):
                                n += 1
                    return n

                def phC_pe(t):
                    p.w(t, "s_act", bC["s_act"])
                    for u in range(NU + 3):
                        if u < NU:
                            tt, hh = divmod(u, H)
                            kv = hh // G
                            if tt == 0:
                                p.w(t, "s_dve", b1ev[0] + KVH + hh + 1)
                            lhs = qbuf[:, hh * T + tt * 128:hh * T + (tt + 1) * 128]
                            p.w(t, "s_act_exp", u)
                            for n2 in range(2 * NS2):
                                o = kv * T + n2 * SWH
                                ins = t.matmul(
                                    scF[:, n2 * SWH:(n2 + 1) * SWH],
                                    lhs, kbuf[:, o:o + SWH],
                                    start=True, stop=True)
                            p.inc(ins, "s_pe_sc")
                        if 2 <= u < NU + 2:
                            up = u - 2
                            p.w(t, "s_dve_p", up + 1)
                            p.w(t, "s_dve_ev", up)
                            for j in range(NT):
                                ins = t.transpose(
                                    ps_t[:, j * 128:(j + 1) * 128],
                                    p_sb[up % 3]()[:, j * 128:(j + 1) * 128],
                                    ident[:])
                            p.inc(ins, "s_pe_tr")
                        if u >= 6 and (u - 6) % G == 0 and (u - 6) // G < NKV:
                            kvi = (u - 6) // G
                            tt2, kv2 = divmod(kvi, KVH)
                            p.w(t, "s_dve_ev", 4 * kvi + 4)
                            p.w(t, "s_act_af", kvi - 1)
                            for j in range(NT):
                                lv = vTb[:, j * KVD + kv2 * 128:
                                         j * KVD + (kv2 + 1) * 128]
                                for g in range(G):
                                    ins = t.matmul(
                                        pvps[kvi % 2][:, g * 128:(g + 1) * 128],
                                        lv,
                                        pTg_of(kvi % 2, g)[:, j * 128:
                                                           (j + 1) * 128],
                                        start=(j == 0 and g == 0),
                                        stop=(j == NT - 1 and g == G - 1))
                            p.inc(ins, "s_pe_pv")
                block.tensor(phC_pe)

                def emit_af(s, kvi):
                    tt2, kv2 = divmod(kvi, KVH)
                    p.w(s, "s_pe_pv", kvi + 1)
                    dst = attfm[:].rearrange("p (h t) -> p h t", t=T)[
                        :, kv2 * G:(kv2 + 1) * G, tt2 * 128:(tt2 + 1) * 128]
                    src = pvps[kvi % 2][:].rearrange("p (g t) -> p g t", t=128)
                    p.inc(s.activation(dst, src, AF.Identity, bias=f32(0.0)),
                          "s_act_af")

                def phC_act(s):
                    for u in range(NU):
                        tt, hh = divmod(u, H)
                        rs = ranges_of(tt)
                        p.w(s, "s_pe_sc", u + 1)
                        p.w(s, "s_dve_p", u - 1)
                        p.w(s, "s_pe_tr", u - 2)
                        last = None
                        for ncol, (lo, hi, bias) in enumerate(rs):
                            last = s.activation(
                                p_sb[u % 3]()[:, lo:hi],
                                scF[:, lo:hi], AF.Exp,
                                bias=f32(bias), scale=f32(D ** -0.5),
                                accum_out=Zp[u % 2][:, ncol:ncol + 1])
                        p.inc(last, "s_act_exp")
                        if u >= 7 and (u - 7) % G == 0 and (u - 7) // G < NKV:
                            emit_af(s, (u - 7) // G)
                    for kvi in range(((NU - 8) // G) + 1, NKV):
                        emit_af(s, kvi)
                block.scalar(phC_act)

                def phC_dve(v):
                    for u in range(NU + 2):
                        if u >= 2:
                            ue = u - 2
                            kvi, g = divmod(ue, G)
                            p.w(v, "s_pe_tr", ue + 1)
                            if g == 0:
                                p.w(v, "s_pe_pv", kvi - 1)
                            p.inc(v.tensor_copy(pTg_of(kvi % 2, g)[:], ps_t[:]),
                                  "s_dve_ev")
                        if u < NU:
                            tt, hh = divmod(u, H)
                            ncol = len(ranges_of(tt))
                            dlo = tt * 128
                            pd = p_sb[u % 3]()[:, dlo:dlo + 128]
                            p.w(v, "s_act_exp", u + 1)
                            p.inc(v.reduce_sum(Zp[u % 2][:, ncol:ncol + 1], pd,
                                               axis=AX.X, negate=True), "s_dve_c")
                            p.w(v, "s_dve_c", 4 * u + 1)
                            p.inc(v.scalar_tensor_tensor(
                                out=pd, in0=pd, scalar=1.0, in1=etril[:],
                                op0=ALU.mult, op1=ALU.mult,
                                accum_out=Zp[u % 2][:, ncol + 1:ncol + 2]),
                                "s_dve_c")
                            p.w(v, "s_dve_c", 4 * u + 2)
                            p.inc(v.reduce_sum(Zs[u % 2][:],
                                               Zp[u % 2][:, 0:ncol + 2],
                                               axis=AX.X), "s_dve_c")
                            p.w(v, "s_dve_c", 4 * u + 3)
                            p.inc(v.reciprocal(rz[u % 2][:], Zs[u % 2][:]),
                                  "s_dve_c")
                            p.w(v, "s_dve_c", 4 * u + 4)
                            p.inc(v.tensor_scalar(out=p_sb[u % 3]()[:],
                                                  in0=p_sb[u % 3]()[:],
                                                  scalar1=rz[u % 2][:],
                                                  scalar2=None,
                                                  op0=ALU.mult), "s_dve_p")
                block.vector(phC_dve)
            esB2.close()

            if stop_after == "C":
                esBIG.close()
                return nc
            full_barrier(skip_sync=True)
            with ExitStack() as esD:
                Dm = lambda n, shp, dt: esD.enter_context(nc.sbuf_tensor(n, shp, dt))
                xblk = [Dm(f"xblk{i}", [128, T], F32) for i in range(2)]
                tmpD = Dm("tmpD", [128, T], F32)
                r1b = [Dm(f"r1b{i}", [128, TH], F32) for i in range(2)]
                r1acc = xn1[:].bitcast(F32)
                pso = [esD.enter_context(nc.psum_tensor(f"pso{i}", [128, T], F32))
                       for i in range(2)]
                bD = dict(p.cnt)
                ldw_bD = [p.cnt["ld_w0"], p.cnt["ld_w1"]]
                ldx_bD = [p.cnt["ld_x0"], p.cnt["ld_x1"]]

                def phD_sync(sync):
                    def wdma(i):
                        p.w(sync, "s_pool", bD["s_pool"] + i - 1)
                        p.inc(sync.dma_start(
                            out=wstg[i % 2][:].rearrange("p (c i) -> p c i", i=128),
                            in_=woT[:, i * 128:(i + 1) * 128]
                                .rearrange("(c p) i -> p c i", p=128)),
                            f"ld_w{i % 2}", 16)
                    wdma(0)
                    wdma(1)
                    for i in range(NCC):
                        if i == 0:
                            p.w(sync, "s_pe_pv", NKV)
                        p.w(sync, "s_dve", bD["s_dve"] + i - 1)
                        p.inc(sync.dma_start(out=xblk[i % 2][:],
                                             in_=xT[i * 128:(i + 1) * 128, :]),
                              f"ld_x{i % 2}", 16)
                        if i + 2 < NCC:
                            wdma(i + 2)
                block.sync(phD_sync)

                def phD_pool(g):
                    for i in range(NCC):
                        p.w(g, "s_dve", bD["s_dve"] + i + 1)
                        p.inc(g.dma_start(out=r1T[i * 128:(i + 1) * 128, :],
                                          in_=r1b[i % 2][:]), f"st_r1{i % 2}", 16)
                block.gpsimd(phD_pool)

                def phD_pe(t):
                    p.w(t, "s_dve_ev", NU)
                    p.w(t, "s_act_af", NKV)
                    for i in range(NCC):
                        p.w(t, "s_pool", bD["s_pool"] + i + 1)
                        p.w(t, "s_act", bD["s_act"] + i - 1)
                        for ca in range(NCC):
                            for ns in range(NS):
                                ins = t.matmul(
                                    pso[i % 2][:, ns * SW:(ns + 1) * SW],
                                    wbf[i % 2][:, ca * 128:(ca + 1) * 128],
                                    attfm[:, ca * T + ns * SW:
                                          ca * T + (ns + 1) * SW],
                                    start=(ca == 0), stop=(ca == NCC - 1))
                        p.inc(ins, "s_pe")
                block.tensor(phD_pe)

                def phD_act(s):
                    for i in range(NCC):
                        p.w(s, "s_pe", bD["s_pe"] + i + 1)
                        p.w(s, "s_dve", bD["s_dve"] + i)
                        p.inc(s.activation(tmpD[:], pso[i % 2][:], AF.Identity,
                                           bias=bof[:, i:i + 1]), "s_act")
                block.scalar(phD_act)

                def phD_dve(v):
                    def radd(i):
                        p.w(v, "s_act", bD["s_act"] + i + 1)
                        p.w(v, f"ld_x{i % 2}", ldx_bD[i % 2] + 16 * (i // 2 + 1))
                        p.w(v, f"st_r1{i % 2}", 16 * (i // 2))
                        v.tensor_add(r1acc[:, i * TH:(i + 1) * TH],
                                     tmpD[:, 0:TH], xblk[i % 2][:, 0:TH])
                        p.inc(v.tensor_add(r1b[i % 2][:], tmpD[:, TH:T],
                                           xblk[i % 2][:, TH:T]), "s_dve")
                    for i in range(NCC):
                        p.w(v, f"ld_w{i % 2}", ldw_bD[i % 2] + 16 * (i // 2 + 1))
                        p.w(v, "s_pe", bD["s_pe"] + i - 1)
                        p.inc(v.tensor_copy(wbf[i % 2][:], wstg[i % 2][:]),
                              "s_pool")
                        if i >= 1:
                            radd(i - 1)
                    radd(NCC - 1)
                block.vector(phD_dve)


            esW.close()
            full_barrier()
            with ExitStack() as esE:
                Em = lambda n, shp, dt: esE.enter_context(nc.sbuf_tensor(n, shp, dt))
                r1acc = xn1[:].bitcast(F32)
                xn2 = qbuf[:, 0:NCC * TH]
                hblk = qbuf[:, NCC * TH:NCC * TH + HBS * TH]
                p1stg = [Em(f"p1stg{i}", [128, C], F32) for i in range(2)]
                p1bf = [Em(f"p1bf{i}", [128, C], BF16) for i in range(2)]
                p2stg = [Em(f"p2stg{i}", [128, HBS * 128], F32) for i in range(2)]
                p2bf = [Em(f"p2bf{i}", [128, HBS * 128], BF16) for i in range(2)]
                sq2 = [Em(f"sq2{i}", [128, TH], BF16) for i in range(2)]
                mb2 = Em("mb2", [128, TH], F32)
                t22 = Em("t22", [128, TH], F32)
                t32 = Em("t32", [128, TH], BF16)
                ab2 = Em("ab2", [128, TH], BF16)
                bma2 = Em("bma2", [128, TH], BF16)
                outsb = [mb2, t22]

                pf_p1 = [0, 0]
                for hf in range(2):
                    if hf == 1:
                        pf_p1 = [p.cnt["ld_p10"], p.cnt["ld_p11"]]

                        def phE_prefetch(sync):
                            p.w(sync, "s_pool", p.cnt["s_pool"])
                            for hc in range(2):
                                p.inc(sync.dma_start(
                                    out=p1stg[hc % 2][:]
                                    .rearrange("p (c i) -> p c i", i=128),
                                    in_=p1T[:, hc * 128:(hc + 1) * 128]
                                    .rearrange("(c p) i -> p c i", p=128)),
                                    f"ld_p1{hc % 2}", 16)
                        block.sync(phE_prefetch)
                    full_barrier(skip_sync=True)
                    tofs = hf * TH
                    bE = dict(p.cnt)

                    if hf == 1:
                        def phE0_sync(sync, bE=bE):
                            p.w(sync, "s_dve", bE["s_dve"])
                            for sl in range(2):
                                p.w(sync, f"st_r1{sl}", 16 * (NCC // 2))
                            p.inc(sync.dma_start(
                                out=r1acc[:].rearrange("p (c t) -> p c t", t=TH),
                                in_=r1T.rearrange("(c p) t -> p c t", p=128)),
                                "ld_r1", 16)
                        block.sync(phE0_sync)

                    with ExitStack() as esE1:
                        psS2 = esE1.enter_context(
                            nc.psum_tensor(f"psS2_{hf}", [128, TH], F32))
                        psSS2 = esE1.enter_context(
                            nc.psum_tensor(f"psSS2_{hf}", [128, TH], F32))

                        def phE1_cast(v, hf=hf, bE=bE):
                            for c in range(NCC):
                                if hf == 1:
                                    p.w(v, "ld_r1", 16)
                                cs = slice(c * TH, (c + 1) * TH)
                                p.inc(v.tensor_copy(xn2[:, cs], r1acc[:, cs]),
                                      "s_pool")
                        block.vector(phE1_cast)

                        def phE1_sq(a, hf=hf, bE=bE):
                            for c in range(NCC):
                                if hf == 1:
                                    p.w(a, "ld_r1", 16)
                                p.w(a, "s_pe", bE["s_pe"] + c - 1)
                                cs = slice(c * TH, (c + 1) * TH)
                                p.inc(a.activation(sq2[c % 2][:], r1acc[:, cs],
                                                   AF.Square), "s_act")
                        block.scalar(phE1_sq)

                        def phE1_pe(t, bE=bE):
                            p.w(t, "s_dve", bE["s_dve"])
                            p.w(t, "s_act", bE["s_act"])
                            for c in range(NCC):
                                p.w(t, "s_pool", bE["s_pool"] + c + 1)
                                p.w(t, "s_act", bE["s_act"] + c + 1)
                                for ns in range(NSH):
                                    sl = slice(ns * SWT, (ns + 1) * SWT)
                                    t.matmul(psS2[:, sl], ones128[:],
                                             xn2[:, c * TH + ns * SWT:
                                                 c * TH + (ns + 1) * SWT],
                                             start=(c == 0), stop=(c == NCC - 1))
                                    ins = t.matmul(psSS2[:, sl], ones128[:],
                                                   sq2[c % 2][:, sl],
                                                   start=(c == 0),
                                                   stop=(c == NCC - 1))
                                p.inc(ins, "s_pe")
                        block.tensor(phE1_pe)

                        def phE2_v1(v, bE=bE):
                            p.w(v, "s_pe", bE["s_pe"] + NCC)
                            p.inc(v.tensor_scalar(out=mb2[:], in0=psS2[:],
                                                  scalar1=1.0 / C, scalar2=None,
                                                  op0=ALU.mult), "s_dve")
                        block.vector(phE2_v1)

                        def phE2_s1(s):
                            p.w(s, "s_dve", p.cnt["s_dve"])
                            p.inc(s.activation(t22[:], mb2[:], AF.Square), "s_act")
                        block.scalar(phE2_s1)

                        def phE2_v2(v):
                            p.w(v, "s_act", p.cnt["s_act"])
                            p.inc(v.scalar_tensor_tensor(
                                out=t22[:], in0=psSS2[:], scalar=1.0 / C,
                                in1=t22[:], op0=ALU.mult, op1=ALU.subtract),
                                "s_dve")
                        block.vector(phE2_v2)

                        def phE2_s2(s):
                            p.w(s, "s_dve", p.cnt["s_dve"])
                            p.inc(s.activation(t22[:], t22[:], AF.Ln,
                                               bias=f32(EPS)), "s_act")
                            p.w(s, "s_act", p.cnt["s_act"])
                            p.inc(s.activation(ab2[:], t22[:], AF.Exp,
                                               scale=f32(-0.5)), "s_act")
                        block.scalar(phE2_s2)

                        def phE2_v3(v):
                            p.w(v, "s_act", p.cnt["s_act"])
                            p.inc(v.scalar_tensor_tensor(
                                out=bma2[:], in0=mb2[:], scalar=-1.0, in1=ab2[:],
                                op0=ALU.mult, op1=ALU.mult), "s_dve")
                            for c in range(NCC):
                                cs = slice(c * TH, (c + 1) * TH)
                                p.w(v, "s_dve", p.cnt["s_dve"])
                                p.inc(v.tensor_mul(t32[:], xn2[:, cs], ab2[:]),
                                      "s_dve")
                                p.w(v, "s_dve", p.cnt["s_dve"])
                                p.inc(v.tensor_add(xn2[:, cs], t32[:], bma2[:]),
                                      "s_dve")
                        block.vector(phE2_v3)

                    full_barrier(skip_sync=True)
                    with ExitStack() as esE4:
                        psh = [esE4.enter_context(
                            nc.psum_tensor(f"psh{hf}_{i}", [128, TH], F32))
                            for i in range(2)]
                        pso2 = [esE4.enter_context(
                            nc.psum_tensor(f"pso2{hf}_{i}", [128, TH], F32))
                            for i in range(2)]
                        bE4 = dict(p.cnt)
                        stob = [p.cnt["st_o0"], p.cnt["st_o1"]]
                        b2f_ap = lambda i: b2f[:, i:i + 1]
                        ldp1 = (pf_p1 if hf == 1
                                else [p.cnt["ld_p10"], p.cnt["ld_p11"]])
                        ldp2 = [p.cnt["ld_p20"], p.cnt["ld_p21"]]

                        def phE4_sync(sync, bE4=bE4, hf=hf):
                            for hb in range(NHB):
                                for hc in range(HBS):
                                    if hf == 1 and hb == 0 and hc < 2:
                                        continue
                                    if hf == 0 and hb == 0 and hc < 2:
                                        p.w(sync, "s_pe_pv", NKV)
                                    else:
                                        p.w(sync, "s_pool",
                                            bE4["s_pool"] + hb * (HBS + NCC)
                                            + hc - 1)
                                    hcg = hb * HBS + hc
                                    p.inc(sync.dma_start(
                                        out=p1stg[hc % 2][:]
                                        .rearrange("p (c i) -> p c i", i=128),
                                        in_=p1T[:, hcg * 128:(hcg + 1) * 128]
                                        .rearrange("(c p) i -> p c i", p=128)),
                                        f"ld_p1{hc % 2}", 16)
                                for i in range(NCC):
                                    if hf == 0 and hb == 0 and i < 2:
                                        p.w(sync, "s_pe_pv", NKV)
                                    else:
                                        p.w(sync, "s_pool",
                                            bE4["s_pool"] + hb * (HBS + NCC)
                                            + HBS + i - 1)
                                    p.inc(sync.dma_start(
                                        out=p2stg[i % 2][:]
                                        .rearrange("p (c j) -> p c j", j=128),
                                        in_=p2T[hb * HBS * 128:
                                                (hb + 1) * HBS * 128,
                                                i * 128:(i + 1) * 128]
                                        .rearrange("(c p) j -> p c j", p=128)),
                                        f"ld_p2{i % 2}", 16)
                        block.sync(phE4_sync)

                        def phE4_cast(v, bE4=bE4, ldp1=ldp1, ldp2=ldp2):
                            pend = []
                            for hb in range(NHB):
                                for hc in range(HBS):
                                    nf = hb * (HBS // 2) + hc // 2 + 1
                                    p.w(v, f"ld_p1{hc % 2}",
                                        ldp1[hc % 2] + 16 * nf)
                                    p.w(v, "s_pe",
                                        bE4["s_pe"] + hb * (HBS + NCC) + hc - 1)
                                    p.inc(v.tensor_copy(p1bf[hc % 2][:],
                                                        p1stg[hc % 2][:]),
                                          "s_pool")
                                    if hc < len(pend):
                                        pend[hc]()
                                pend = []

                                def evac2(i, hb=hb):
                                    p.w(v, "s_pe",
                                        bE4["s_pe"] + hb * (HBS + NCC)
                                        + HBS + i + 1)
                                    p.w(v, "s_dve",
                                        bE4["s_dve"] + hb * NCC + i)
                                    cs = slice(i * TH, (i + 1) * TH)
                                    if hb == NHB - 1:
                                        p.w(v, f"st_o{i % 2}",
                                            stob[i % 2] + 16 * (i // 2))
                                        p.inc(v.scalar_tensor_tensor(
                                            out=outsb[i % 2][:],
                                            in0=pso2[i % 2][:],
                                            scalar=b2f_ap(i),
                                            in1=r1acc[:, cs],
                                            op0=ALU.add, op1=ALU.add), "s_dve")
                                    else:
                                        p.inc(v.tensor_add(r1acc[:, cs],
                                                           pso2[i % 2][:],
                                                           r1acc[:, cs]), "s_dve")
                                for i in range(NCC):
                                    nf = hb * (NCC // 2) + i // 2 + 1
                                    p.w(v, f"ld_p2{i % 2}",
                                        ldp2[i % 2] + 16 * nf)
                                    p.w(v, "s_pe",
                                        bE4["s_pe"] + hb * (HBS + NCC)
                                        + HBS + i - 1)
                                    p.inc(v.tensor_copy(p2bf[i % 2][:],
                                                        p2stg[i % 2][:]),
                                          "s_pool")
                                    if i >= 2:
                                        evac2(i - 2)
                                if hb == NHB - 1:
                                    evac2(NCC - 2)
                                    evac2(NCC - 1)
                                else:
                                    pend = [lambda i=NCC - 2, hb=hb: evac2(i, hb),
                                            lambda i=NCC - 1, hb=hb: evac2(i, hb)]
                        block.vector(phE4_cast)

                        def phE4_pe(t, bE4=bE4):
                            for hb in range(NHB):
                                for hc in range(HBS):
                                    p.w(t, "s_pool",
                                        bE4["s_pool"] + hb * (HBS + NCC) + hc + 1)
                                    p.w(t, "s_act",
                                        bE4["s_act"] + hb * HBS + hc - 1)
                                    for c in range(NCC):
                                        for ns in range(NSH):
                                            sl = slice(ns * SWT, (ns + 1) * SWT)
                                            ins = t.matmul(
                                                psh[hc % 2][:, sl],
                                                p1bf[hc % 2][:,
                                                             c * 128:(c + 1) * 128],
                                                xn2[:, c * TH + ns * SWT:
                                                    c * TH + (ns + 1) * SWT],
                                                start=(c == 0),
                                                stop=(c == NCC - 1))
                                    p.inc(ins, "s_pe")
                                for i in range(NCC):
                                    p.w(t, "s_pool",
                                        bE4["s_pool"] + hb * (HBS + NCC)
                                        + HBS + i + 1)
                                    p.w(t, "s_act",
                                        bE4["s_act"] + (hb + 1) * HBS)
                                    p.w(t, "s_dve",
                                        bE4["s_dve"] + hb * NCC + i - 1)
                                    for hc in range(HBS):
                                        for ns in range(NSH):
                                            sl = slice(ns * SWT, (ns + 1) * SWT)
                                            ins = t.matmul(
                                                pso2[i % 2][:, sl],
                                                p2bf[i % 2][:,
                                                            hc * 128:(hc + 1) * 128],
                                                hblk[:, hc * TH + ns * SWT:
                                                     hc * TH + (ns + 1) * SWT],
                                                start=(hc == 0),
                                                stop=(hc == HBS - 1))
                                    p.inc(ins, "s_pe")
                        block.tensor(phE4_pe)

                        def phE4_act(s, bE4=bE4):
                            for hb in range(NHB):
                                for hc in range(HBS):
                                    p.w(s, "s_pe",
                                        bE4["s_pe"] + hb * (HBS + NCC) + hc + 1)
                                    p.inc(s.activation(
                                        hblk[:, hc * TH:(hc + 1) * TH],
                                        psh[hc % 2][:], AF.Gelu,
                                        bias=b1f[:, hb * HBS + hc:
                                                 hb * HBS + hc + 1]), "s_act")
                        block.scalar(phE4_act)


                        def phE5_st(g, bE4=bE4, stob=stob, tofs=tofs):
                            for i in range(NCC):
                                p.w(g, "s_dve",
                                    bE4["s_dve"] + (NHB - 1) * NCC + i + 1)
                                p.inc(g.dma_start(
                                    out=outT[i * 128:(i + 1) * 128,
                                             tofs:tofs + TH],
                                    in_=outsb[i % 2][:]), f"st_o{i % 2}", 16)
                        block.gpsimd(phE5_st)

                def fin(sync):
                    p.w(sync, "st_o0", p.cnt["st_o0"])
                    p.w(sync, "st_o1", p.cnt["st_o1"])
                block.sync(fin)
        esBIG.close()
    return nc


def _host_tables(H):
    D = 128
    inv = 1.0 / 10000.0 ** (np.arange(0, D, 2, dtype=np.float64) / D)
    hidx = np.arange(H, dtype=np.float64)
    freqs = np.outer(hidx, inv)
    emb = np.concatenate([freqs, freqs], axis=1)
    cos_t = np.ascontiguousarray(np.cos(emb).T.astype(np.float32))
    sin = np.sin(emb).T.astype(np.float32)
    ssin = sin.copy()
    ssin[:64] = -sin[:64]
    ssin = np.ascontiguousarray(ssin)
    tril = np.tril(np.ones((128, 128), dtype=np.float32))
    etril = np.where(tril > 0, np.float32(np.e),
                     np.float32(1.0)).astype(np.float32)
    ident = np.eye(128, dtype=np.float32)
    return cos_t, ssin, etril, ident


def kernel(L0_attn_x, L0_attn_wq, L0_attn_bq, L0_attn_wk, L0_attn_bk,
           L0_attn_wv, L0_attn_bv, L0_attn_wo, L0_attn_bo,
           L0_ffn_proj1, L0_ffn_proj1_bias, L0_ffn_proj2, L0_ffn_proj2_bias,
           L0_ffn_gate_w):
    x = np.asarray(L0_attn_x)
    Bn, T, C = x.shape
    H = 16
    KVH = 4
    HF = np.asarray(L0_ffn_proj1).shape[0]
    KVD = KVH * 128
    assert C == H * 128

    cos_t, ssin_t, etril, ident = _host_tables(H)

    def fm(b):
        return np.ascontiguousarray(
            np.asarray(b).reshape(-1, 128).T.astype(np.float32))

    common = {
        "wqT": np.ascontiguousarray(np.asarray(L0_attn_wq).T),
        "wkT": np.ascontiguousarray(np.asarray(L0_attn_wk).T),
        "wvT": np.ascontiguousarray(np.asarray(L0_attn_wv).T),
        "woT": np.ascontiguousarray(np.asarray(L0_attn_wo).T),
        "p1T": np.ascontiguousarray(np.asarray(L0_ffn_proj1).T),
        "p2T": np.ascontiguousarray(np.asarray(L0_ffn_proj2).T),
        "bq_fm": fm(L0_attn_bq),
        "bk_fm": fm(L0_attn_bk),
        "bv_row": np.ascontiguousarray(
            np.asarray(L0_attn_bv).reshape(1, KVD).astype(np.float32)),
        "bo_fm": fm(L0_attn_bo),
        "b1_fm": fm(L0_ffn_proj1_bias),
        "b2_fm": fm(L0_ffn_proj2_bias),
        "cos_t": cos_t, "ssin_t": ssin_t, "etril": etril, "ident": ident,
        "gate_w": np.ascontiguousarray(
            np.asarray(L0_ffn_gate_w).reshape(-1, 128, 8)
            .transpose(1, 0, 2).reshape(128, -1).astype(np.float32)),
    }
    in_maps = []
    for b in range(Bn):
        m = dict(common)
        m["xT"] = np.ascontiguousarray(x[b].T)
        in_maps.append(m)

    nc = build_nc(T=T, H=H, KVH=KVH, HF=HF)
    res = run_bass_kernel_spmd(nc, in_maps, core_ids=list(range(Bn)))
    out = np.empty((Bn, T, C), dtype=np.float32)
    for b in range(Bn):
        out[b] = res.results[b]["outT"].T
    return out


# version 18 (final)
# speedup vs baseline: 1.0425x; 1.0397x over previous
import sys
sys.path.insert(0, "/opt/trn_rl_repo")
from contextlib import ExitStack
import numpy as np
import ml_dtypes
import concourse.bass as bass
import concourse.mybir as mybir
from concourse.bass_utils import run_bass_kernel_spmd

F32 = mybir.dt.float32
BF16 = mybir.dt.bfloat16
AF = mybir.ActivationFunctionType
ALU = mybir.AluOpType
AX = mybir.AxisListType
f32 = np.float32

EPS = 1e-5
MB = 20.0


class P:
    def __init__(self, nc):
        self.nc = nc
        self.sem = {}
        self.cnt = {}

    def new_sem(self, es, name):
        self.sem[name] = es.enter_context(self.nc.semaphore(name))
        self.cnt[name] = 0

    def inc(self, ins, name, n=1):
        self.cnt[name] += n
        ins.then_inc(self.sem[name], n)
        return ins

    def w(self, eng, name, val):
        if val > 0:
            eng.wait_ge(self.sem[name], val)


def build_nc(T=2048, H=16, KVH=4, HF=8192, stop_after=None):
    D = 128
    C = H * D
    G = H // KVH
    NCC = C // 128
    NT = T // 128
    NHC = HF // 128
    HBS = min(16, NHC)
    NHB = NHC // HBS
    TH = T // 2
    SW = min(512, T)
    SWH = min(512, T // 2)
    SWT = min(512, TH)
    NS = T // SW
    NS2 = (T // 2) // SWH
    NSH = TH // SWT
    KVD = KVH * D
    NU = NT * H
    NKV = NT * KVH
    assert T % 256 == 0 and H % KVH == 0 and NHC % HBS == 0 and NCC == H

    nc = bass.Bass("TRN2", target_bir_lowering=False)

    xT = nc.dram_tensor("xT", [C, T], F32, kind="ExternalInput").ap()
    wqT = nc.dram_tensor("wqT", [C, C], F32, kind="ExternalInput").ap()
    wkT = nc.dram_tensor("wkT", [C, KVD], F32, kind="ExternalInput").ap()
    wvT = nc.dram_tensor("wvT", [C, KVD], F32, kind="ExternalInput").ap()
    woT = nc.dram_tensor("woT", [C, C], F32, kind="ExternalInput").ap()
    p1T = nc.dram_tensor("p1T", [C, HF], F32, kind="ExternalInput").ap()
    p2T = nc.dram_tensor("p2T", [HF, C], F32, kind="ExternalInput").ap()
    bq_h = nc.dram_tensor("bq_fm", [128, H], F32, kind="ExternalInput").ap()
    bk_h = nc.dram_tensor("bk_fm", [128, KVH], F32, kind="ExternalInput").ap()
    bv_h = nc.dram_tensor("bv_row", [1, KVD], F32, kind="ExternalInput").ap()
    bo_h = nc.dram_tensor("bo_fm", [128, NCC], F32, kind="ExternalInput").ap()
    b1_h = nc.dram_tensor("b1_fm", [128, NHC], F32, kind="ExternalInput").ap()
    b2_h = nc.dram_tensor("b2_fm", [128, NCC], F32, kind="ExternalInput").ap()
    cos_h = nc.dram_tensor("cos_t", [128, H], F32, kind="ExternalInput").ap()
    ssin_h = nc.dram_tensor("ssin_t", [128, H], F32, kind="ExternalInput").ap()
    etril_h = nc.dram_tensor("etril", [128, 128], F32, kind="ExternalInput").ap()
    ident_h = nc.dram_tensor("ident", [128, 128], F32, kind="ExternalInput").ap()
    gate_h = nc.dram_tensor("gate_w", [128, (C // 128) * 8], F32,
                            kind="ExternalInput").ap()
    r1T = nc.dram_tensor("r1T", [C, T // 2], F32, kind="Internal").ap()
    outT = nc.dram_tensor("outT", [C, T], F32, kind="ExternalOutput").ap()

    p = P(nc)
    es0 = ExitStack()
    with es0:
        for s in ["ld_c", "ld_x0", "ld_x1", "ld_xg", "ld_w0", "ld_w1",
                  "ld_p10", "ld_p11", "ld_p20", "ld_p21", "ld_r1",
                  "st_r10", "st_r11", "st_o0", "st_o1",
                  "s_pool", "s_pe", "s_act", "s_dve",
                  "s_pe_sc", "s_pe_tr", "s_pe_pv",
                  "s_act_exp", "s_act_af", "s_dve_c", "s_dve_p", "s_dve_ev"]:
            p.new_sem(es0, s)

        SB = lambda n, shp, dt: es0.enter_context(nc.sbuf_tensor(n, shp, dt))

        etril = SB("etril_s", [128, 128], F32)
        ident = SB("ident_s", [128, 128], BF16)
        identf = SB("identf_s", [128, 128], F32)
        cos_s = SB("cos_s", [128, H], F32)
        ssin_s = SB("ssin_s", [128, H], F32)
        bqf = SB("bqf", [128, H], F32)
        bkf = SB("bkf", [128, KVH], F32)
        bvr = SB("bvr", [1, KVD], F32)
        bvr_bf = SB("bvr_bf", [1, KVD], BF16)
        ones1 = SB("ones1", [1, 128], BF16)
        ones128 = SB("ones128", [128, 128], BF16)
        bof = SB("bof", [128, NCC], F32)
        b1f = SB("b1f", [128, NHC], F32)
        b2f = SB("b2f", [128, NCC], F32)
        bqcos = SB("bqcos", [128, H], F32)
        bqssin = SB("bqssin", [128, H], F32)
        bkcos = SB("bkcos", [128, KVH], F32)
        bkssin = SB("bkssin", [128, KVH], F32)
        shq = SB("shq", [128, H], F32)
        shk = SB("shk", [128, KVH], F32)
        gate_s = SB("gate_s", [128, (C // 128) * 8], F32)
        Zp = [SB(f"Zp{i}", [128, 8], F32) for i in range(2)]
        Zs = [SB(f"Zs{i}", [128, 1], F32) for i in range(2)]
        rz = [SB(f"rz{i}", [128, 1], F32) for i in range(2)]

        esBIG = ExitStack()
        qbuf = esBIG.enter_context(nc.sbuf_tensor("qbuf", [128, H * T], BF16))
        xn1 = esBIG.enter_context(nc.sbuf_tensor("xn1", [128, NCC * T], BF16))
        xb = qbuf
        attfm = qbuf
        p_sb = [lambda i=i: xn1[:, i * T:(i + 1) * T] for i in range(3)]
        pTg_of = lambda sl, g: xn1[:, (3 + sl * G + g) * T:(4 + sl * G + g) * T]

        esB2 = ExitStack()

        with nc.Block() as block:
            def full_barrier(skip_sync=False, skip=()):
                tgt = {k: v for k, v in p.cnt.items() if v > 0}
                regs = {"gpsimd": block.gpsimd, "tensor": block.tensor,
                        "scalar": block.scalar, "vector": block.vector}
                if not skip_sync:
                    regs["sync"] = block.sync
                for nm, reg in regs.items():
                    if nm in skip:
                        continue
                    def _bar(eng, tgt=tgt):
                        for k, v in tgt.items():
                            eng.wait_ge(p.sem[k], v)
                    reg(_bar)

            def ph0_sync(sync):
                for dst, src in [(etril[:], etril_h), (identf[:], ident_h),
                                 (gate_s[:], gate_h),
                                 (cos_s[:], cos_h), (ssin_s[:], ssin_h),
                                 (bqf[:], bq_h), (bkf[:], bk_h), (bvr[:], bv_h),
                                 (bof[:], bo_h), (b1f[:], b1_h), (b2f[:], b2_h)]:
                    p.inc(sync.dma_start(out=dst, in_=src), "ld_c", 16)
            block.sync(ph0_sync)

            def ph0_pool(g):
                p.w(g, "ld_c", p.cnt["ld_c"])
                g.memset(ones1[:], 1.0)
                g.memset(ones128[:], 1.0)
                g.tensor_copy(ident[:], identf[:])
                p.inc(g.tensor_copy(bvr_bf[:], bvr[:]), "s_pool")
            block.gpsimd(ph0_pool)

            def ph0_dve(v):
                p.w(v, "ld_c", p.cnt["ld_c"])
                v.tensor_mul(bqcos[:], bqf[:], cos_s[:])
                v.tensor_copy(shq[0:64, :], bqf[64:128, :])
                p.inc(v.tensor_copy(shq[64:128, :], bqf[0:64, :]), "s_dve")
                p.w(v, "s_dve", p.cnt["s_dve"])
                v.tensor_mul(bqssin[:], shq[:], ssin_s[:])
                v.tensor_mul(bkcos[:], bkf[:], cos_s[:, 0:KVH])
                v.tensor_copy(shk[0:64, :], bkf[64:128, :])
                p.inc(v.tensor_copy(shk[64:128, :], bkf[0:64, :]), "s_dve")
                p.w(v, "s_dve", p.cnt["s_dve"])
                p.inc(v.tensor_mul(bkssin[:], shk[:], ssin_s[:, 0:KVH]), "s_dve")
            block.vector(ph0_dve)

            esW = ExitStack()
            wstg = [esW.enter_context(nc.sbuf_tensor(f"wstg{i}", [128, C], F32))
                    for i in range(2)]
            wbf = [esW.enter_context(nc.sbuf_tensor(f"wbf{i}", [128, C], BF16))
                   for i in range(2)]

            full_barrier()
            with ExitStack() as esA:
                A = lambda n, shp, dt: esA.enter_context(nc.sbuf_tensor(n, shp, dt))
                xstg = [A(f"xstg{i}", [128, T], F32) for i in range(2)]
                sq = [A(f"sq{i}", [128, T], BF16) for i in range(2)]
                mb_b = A("mb_b", [128, T], F32)
                t2_b = A("t2_b", [128, T], F32)
                ab_b = A("ab_b", [128, T], BF16)
                bma_b = A("bma_b", [128, T], BF16)
                psS = esA.enter_context(nc.psum_tensor("psS", [128, T], F32))
                psSS = esA.enter_context(nc.psum_tensor("psSS", [128, T], F32))
                bA = dict(p.cnt)

                def phA_sync(sync):
                    for c in range(0, NCC, 2):
                        p.w(sync, "s_pool", bA["s_pool"] + c - 1)
                        p.w(sync, "s_act", bA["s_act"] + c - 1)
                        p.inc(sync.dma_start(out=xstg[0][:],
                                             in_=xT[c * 128:(c + 1) * 128, :]),
                              "ld_x0", 16)
                block.sync(phA_sync)

                def phA_syncb(g):
                    for c in range(1, NCC, 2):
                        p.w(g, "s_pool", bA["s_pool"] + c - 1)
                        p.w(g, "s_act", bA["s_act"] + c - 1)
                        p.inc(g.dma_start(out=xstg[1][:],
                                          in_=xT[c * 128:(c + 1) * 128, :]),
                              "ld_xg", 16)
                block.gpsimd(phA_syncb)

                def phA_v0(v):
                    for c in range(NCC):
                        p.w(v, "ld_x0" if c % 2 == 0 else "ld_xg",
                            16 * (c // 2 + 1))
                        p.inc(v.tensor_copy(xb[:, c * T:(c + 1) * T],
                                            xstg[c % 2][:]), "s_pool")
                block.vector(phA_v0)

                def phA_sq(a):
                    for c in range(NCC):
                        p.w(a, "ld_x0" if c % 2 == 0 else "ld_xg",
                            16 * (c // 2 + 1))
                        p.w(a, "s_pe", bA["s_pe"] + c - 1)
                        p.inc(a.activation(sq[c % 2][:], xstg[c % 2][:],
                                           AF.Square), "s_act")
                block.scalar(phA_sq)

                def phA_pe(t):
                    for c in range(NCC):
                        p.w(t, "s_pool", bA["s_pool"] + c + 1)
                        p.w(t, "s_act", bA["s_act"] + c + 1)
                        for ns in range(NS):
                            sl = slice(ns * SW, (ns + 1) * SW)
                            t.matmul(psS[:, sl], ones128[:],
                                     xb[:, c * T + ns * SW:c * T + (ns + 1) * SW],
                                     start=(c == 0), stop=(c == NCC - 1))
                            ins = t.matmul(psSS[:, sl], ones128[:], sq[c % 2][:, sl],
                                           start=(c == 0), stop=(c == NCC - 1))
                        p.inc(ins, "s_pe")
                block.tensor(phA_pe)

                def phA_v1(v):
                    p.w(v, "s_pe", bA["s_pe"] + NCC)
                    p.inc(v.tensor_scalar(out=mb_b[:], in0=psS[:], scalar1=1.0 / C,
                                          scalar2=None, op0=ALU.mult), "s_dve")
                block.vector(phA_v1)

                def phA_s1(s):
                    p.w(s, "s_dve", p.cnt["s_dve"])
                    p.inc(s.activation(t2_b[:], mb_b[:], AF.Square), "s_act")
                block.scalar(phA_s1)

                def phA_v2(v):
                    p.w(v, "s_act", p.cnt["s_act"])
                    p.inc(v.scalar_tensor_tensor(out=t2_b[:], in0=psSS[:],
                                                 scalar=1.0 / C, in1=t2_b[:],
                                                 op0=ALU.mult, op1=ALU.subtract),
                          "s_dve")
                block.vector(phA_v2)

                def phA_s2(s):
                    p.w(s, "s_dve", p.cnt["s_dve"])
                    p.inc(s.activation(t2_b[:], t2_b[:], AF.Ln, bias=f32(EPS)),
                          "s_act")
                    p.w(s, "s_act", p.cnt["s_act"])
                    p.inc(s.activation(ab_b[:], t2_b[:], AF.Exp, scale=f32(-0.5)),
                          "s_act")
                block.scalar(phA_s2)

                def phA_v3(v):
                    p.w(v, "s_act", p.cnt["s_act"])
                    p.inc(v.scalar_tensor_tensor(out=bma_b[:], in0=mb_b[:],
                                                 scalar=-1.0, in1=ab_b[:],
                                                 op0=ALU.mult, op1=ALU.mult),
                          "s_dve")
                    for c in range(NCC):
                        cs = slice(c * T, (c + 1) * T)
                        p.w(v, "s_dve", p.cnt["s_dve"])
                        p.inc(v.tensor_mul(xn1[:, cs], xb[:, cs], ab_b[:]),
                              "s_dve")
                        p.w(v, "s_dve", p.cnt["s_dve"])
                        p.inc(v.tensor_add(xn1[:, cs], xn1[:, cs], bma_b[:]),
                              "s_dve")
                block.vector(phA_v3)

            b1ev = [0]
            kbuf = esB2.enter_context(
                nc.sbuf_tensor("kbuf", [128, KVH * T], BF16))
            vTb = esB2.enter_context(
                nc.sbuf_tensor("vTb", [128, NT * KVD], BF16))
            full_barrier(skip_sync=True, skip=("tensor", "scalar"))
            with ExitStack() as esB:
                Bm = lambda n, shp, dt: esB.enter_context(nc.sbuf_tensor(n, shp, dt))

                with ExitStack() as esB0:
                    CPS = (128 * C) // (128 * KVD)
                    NQP = NCC // CPS
                    TPG = min(8, NT)
                    NPASS = NT // TPG
                    NL = NPASS * NQP
                    assert NT % TPG == 0
                    psvt = [esB0.enter_context(
                        nc.psum_tensor(f"psvt{i}", [128, KVD], F32))
                        for i in range(TPG)]
                    bB0 = dict(p.cnt)

                    def phB0_sync(sync):
                        for q in range(NL):
                            qq = q % NQP
                            p.w(sync, "s_pool", bB0["s_pool"] + q - 1)
                            p.inc(sync.dma_start(
                                out=wstg[q % 2][:, 0:CPS * KVD]
                                .rearrange("p (c i) -> p c i", i=KVD),
                                in_=wvT[qq * CPS * 128:(qq + 1) * CPS * 128, :]
                                .rearrange("(c p) i -> p c i", p=128)),
                                f"ld_w{q % 2}", 16)
                    block.sync(phB0_sync)

                    def phB0_sc(s):
                        def cast(i):
                            p.w(s, f"ld_w{i % 2}", 16 * (i // 2 + 1))
                            if i >= 2:
                                p.w(s, "s_pe", bB0["s_pe"] + i - 1)
                            p.inc(s.activation(wbf[i % 2][:, 0:CPS * KVD],
                                               wstg[i % 2][:, 0:CPS * KVD],
                                               AF.Identity, bias=f32(0.0)),
                                  "s_pool")

                        def evac(j):
                            p.w(s, "s_pe",
                                bB0["s_pe"] + (j // TPG + 1) * NQP)
                            p.inc(s.activation(vTb[:, j * KVD:(j + 1) * KVD],
                                               psvt[j % TPG][:], AF.Identity,
                                               bias=f32(0.0)), "s_act")
                        for hv in range(NPASS):
                            for qq in range(NQP):
                                cast(hv * NQP + qq)
                                if qq == 1 and hv >= 1:
                                    for j in range((hv - 1) * TPG, hv * TPG):
                                        evac(j)
                        for j in range((NPASS - 1) * TPG, NPASS * TPG):
                            evac(j)
                    block.scalar(phB0_sc)

                    def phB0_pe(t):
                        for g in range(NL):
                            hv, qq = divmod(g, NQP)
                            p.w(t, "s_pool", bB0["s_pool"] + g + 1)
                            p.w(t, "s_dve",
                                bA["s_dve"] + 5 + 2 * (CPS * qq + CPS - 1))
                            if qq == 0 and hv > 0:
                                p.w(t, "s_act", bB0["s_act"] + hv * TPG)
                            for j in range(hv * TPG, (hv + 1) * TPG):
                                for ci in range(CPS):
                                    c = qq * CPS + ci
                                    ins = t.matmul(
                                        psvt[j % TPG][:],
                                        xn1[:, c * T + j * 128:
                                            c * T + (j + 1) * 128],
                                        wbf[g % 2][:, ci * KVD:(ci + 1) * KVD],
                                        start=(c == 0), stop=False)
                            if qq == NQP - 1:
                                for j in range(hv * TPG, (hv + 1) * TPG):
                                    ins = t.matmul(psvt[j % TPG][:], ones1[:],
                                                   bvr_bf[:], start=False,
                                                   stop=True)
                            p.inc(ins, "s_pe")
                    block.tensor(phB0_pe)

                full_barrier(skip_sync=True)
                with ExitStack() as esB1:
                    B1 = lambda n, shp, dt: esB1.enter_context(
                        nc.sbuf_tensor(n, shp, dt))
                    qc_b = B1("qc_b", [128, T], F32)
                    tmp_b = B1("tmp_b", [128, T], F32)
                    psqk = [esB1.enter_context(
                        nc.psum_tensor(f"psqk{i}", [128, T], F32))
                        for i in range(2)]
                    bB = dict(p.cnt)
                    ldw_b = [p.cnt["ld_w0"], p.cnt["ld_w1"]]
                    NOC = KVH + H

                    def phB_sync(sync):
                        for oc in range(NOC):
                            src, idx = (wkT, oc) if oc < KVH else (wqT, oc - KVH)
                            p.w(sync, "s_pool", bB["s_pool"] + oc - 1)
                            p.inc(sync.dma_start(
                                out=wstg[oc % 2][:]
                                .rearrange("p (c i) -> p c i", i=128),
                                in_=src[:, idx * 128:(idx + 1) * 128]
                                .rearrange("(c p) i -> p c i", p=128)),
                                f"ld_w{oc % 2}", 16)
                    block.sync(phB_sync)


                    b1ev[0] = bB["s_dve"]

                    def phB_pe(t):
                        p.w(t, "s_act", bB["s_act"])
                        for oc in range(NOC):
                            p.w(t, "s_pool", bB["s_pool"] + oc + 1)
                            p.w(t, "s_act", bB["s_act"] + oc - 1)
                            for c in range(NCC):
                                for ns in range(NS):
                                    ins = t.matmul(
                                        psqk[oc % 2][:, ns * SW:(ns + 1) * SW],
                                        wbf[oc % 2][:, c * 128:(c + 1) * 128],
                                        xn1[:, c * T + ns * SW:
                                            c * T + (ns + 1) * SW],
                                        start=(c == 0), stop=(c == NCC - 1))
                            p.inc(ins, "s_pe")
                    block.tensor(phB_pe)

                    def phB_act(s):
                        for oc in range(NOC):
                            isk = oc < KVH
                            idx = oc if isk else oc - KVH
                            ct = bkcos if isk else bqcos
                            st_ = bkssin if isk else bqssin
                            p.w(s, "s_pe", bB["s_pe"] + oc + 1)
                            p.w(s, "s_dve", bB["s_dve"] + oc)
                            s.activation(qc_b[:], psqk[oc % 2][:], AF.Identity,
                                         bias=ct[:, idx:idx + 1],
                                         scale=cos_s[:, idx:idx + 1])
                            s.activation(tmp_b[0:64, :], psqk[oc % 2][64:128, :],
                                         AF.Identity,
                                         bias=st_[0:64, idx:idx + 1],
                                         scale=ssin_s[0:64, idx:idx + 1])
                            p.inc(s.activation(tmp_b[64:128, :],
                                               psqk[oc % 2][0:64, :], AF.Identity,
                                               bias=st_[64:128, idx:idx + 1],
                                               scale=ssin_s[64:128, idx:idx + 1]),
                                  "s_act")
                    block.scalar(phB_act)

                    def phB_dve(v):
                        def evac(oc):
                            isk = oc < KVH
                            idx = oc if isk else oc - KVH
                            dst = (kbuf[:, idx * T:(idx + 1) * T] if isk
                                   else qbuf[:, idx * T:(idx + 1) * T])
                            p.w(v, "s_act", bB["s_act"] + oc + 1)
                            p.inc(v.tensor_add(dst, qc_b[:], tmp_b[:]), "s_dve")
                        for oc in range(NOC):
                            p.w(v, f"ld_w{oc % 2}",
                                ldw_b[oc % 2] + 16 * (oc // 2 + 1))
                            p.w(v, "s_pe", bB["s_pe"] + max(oc - 1, 0))
                            p.inc(v.tensor_copy(wbf[oc % 2][:], wstg[oc % 2][:]),
                                  "s_pool")
                            if oc >= 1:
                                evac(oc - 1)
                        evac(NOC - 1)
                    block.vector(phB_dve)

            if stop_after == "B":
                esB2.close(); esBIG.close()
                return nc
            full_barrier(skip_sync=True)
            with ExitStack() as esC:
                sc = [esC.enter_context(
                    nc.psum_tensor(f"sc{i}", [128, T // 2], F32))
                    for i in range(2)]
                ps_t = esC.enter_context(nc.psum_tensor("ps_t", [128, T], BF16))
                pvps = [esC.enter_context(
                    nc.psum_tensor(f"pvps{i}", [128, G * 128], F32))
                    for i in range(2)]
                bC = dict(p.cnt)

                def ranges_of(tt):
                    b = tt * 128
                    r = []
                    if b > 0:
                        r.append((0, b, -MB + 1.0))
                    r.append((b, T, -MB))
                    return r

                def n_accum_cols(tt):
                    n = 0
                    for sh in range(2):
                        h0, h1 = sh * (T // 2), (sh + 1) * (T // 2)
                        for (lo, hi, _bias) in ranges_of(tt):
                            if max(lo, h0) < min(hi, h1):
                                n += 1
                    return n

                def phC_pe(t):
                    p.w(t, "s_act", bC["s_act"])
                    for u in range(NU + 3):
                        if u < NU:
                            tt, hh = divmod(u, H)
                            kv = hh // G
                            if tt == 0:
                                p.w(t, "s_dve", b1ev[0] + KVH + hh + 1)
                            lhs = qbuf[:, hh * T + tt * 128:hh * T + (tt + 1) * 128]
                            for sh in range(2):
                                p.w(t, "s_act_exp", 2 * u - 1 + sh)
                                for n2 in range(NS2):
                                    o = kv * T + sh * (T // 2) + n2 * SWH
                                    ins = t.matmul(
                                        sc[sh][:, n2 * SWH:(n2 + 1) * SWH],
                                        lhs, kbuf[:, o:o + SWH],
                                        start=True, stop=True)
                                p.inc(ins, "s_pe_sc")
                        if 2 <= u < NU + 2:
                            up = u - 2
                            p.w(t, "s_dve_p", up + 1)
                            p.w(t, "s_dve_ev", up)
                            for j in range(NT):
                                ins = t.transpose(
                                    ps_t[:, j * 128:(j + 1) * 128],
                                    p_sb[up % 3]()[:, j * 128:(j + 1) * 128],
                                    ident[:])
                            p.inc(ins, "s_pe_tr")
                        if u >= 6 and (u - 6) % G == 0 and (u - 6) // G < NKV:
                            kvi = (u - 6) // G
                            tt2, kv2 = divmod(kvi, KVH)
                            p.w(t, "s_dve_ev", 4 * kvi + 4)
                            p.w(t, "s_act_af", kvi - 1)
                            for j in range(NT):
                                lv = vTb[:, j * KVD + kv2 * 128:
                                         j * KVD + (kv2 + 1) * 128]
                                for g in range(G):
                                    ins = t.matmul(
                                        pvps[kvi % 2][:, g * 128:(g + 1) * 128],
                                        lv,
                                        pTg_of(kvi % 2, g)[:, j * 128:
                                                           (j + 1) * 128],
                                        start=(j == 0 and g == 0),
                                        stop=(j == NT - 1 and g == G - 1))
                            p.inc(ins, "s_pe_pv")
                block.tensor(phC_pe)

                def emit_af(s, kvi):
                    tt2, kv2 = divmod(kvi, KVH)
                    p.w(s, "s_pe_pv", kvi + 1)
                    dst = attfm[:].rearrange("p (h t) -> p h t", t=T)[
                        :, kv2 * G:(kv2 + 1) * G, tt2 * 128:(tt2 + 1) * 128]
                    src = pvps[kvi % 2][:].rearrange("p (g t) -> p g t", t=128)
                    p.inc(s.activation(dst, src, AF.Identity, bias=f32(0.0)),
                          "s_act_af")

                def phC_act(s):
                    for u in range(NU):
                        tt, hh = divmod(u, H)
                        rs = ranges_of(tt)
                        ncol = 0
                        for sh in range(2):
                            h0, h1 = sh * (T // 2), (sh + 1) * (T // 2)
                            p.w(s, "s_pe_sc", 2 * u + 1 + sh)
                            if sh == 0:
                                p.w(s, "s_dve_p", u - 1)
                                p.w(s, "s_pe_tr", u - 2)
                            last = None
                            for (lo, hi, bias) in rs:
                                il, ih = max(lo, h0), min(hi, h1)
                                if il >= ih:
                                    continue
                                acc = Zp[u % 2][:, ncol:ncol + 1]
                                ncol += 1
                                last = s.activation(
                                    p_sb[u % 3]()[:, il:ih],
                                    sc[sh][:, il - h0:ih - h0], AF.Exp,
                                    bias=f32(bias), scale=f32(D ** -0.5),
                                    accum_out=acc)
                            p.inc(last, "s_act_exp")
                        if u >= 7 and (u - 7) % G == 0 and (u - 7) // G < NKV:
                            emit_af(s, (u - 7) // G)
                    for kvi in range(((NU - 8) // G) + 1, NKV):
                        emit_af(s, kvi)
                block.scalar(phC_act)

                def phC_dve(v):
                    for u in range(NU + 2):
                        if u >= 2:
                            ue = u - 2
                            kvi, g = divmod(ue, G)
                            p.w(v, "s_pe_tr", ue + 1)
                            if g == 0:
                                p.w(v, "s_pe_pv", kvi - 1)
                            p.inc(v.tensor_copy(pTg_of(kvi % 2, g)[:], ps_t[:]),
                                  "s_dve_ev")
                        if u < NU:
                            tt, hh = divmod(u, H)
                            ncol = n_accum_cols(tt)
                            dlo = tt * 128
                            pd = p_sb[u % 3]()[:, dlo:dlo + 128]
                            p.w(v, "s_act_exp", 2 * u + 2)
                            p.inc(v.reduce_sum(Zp[u % 2][:, ncol:ncol + 1], pd,
                                               axis=AX.X, negate=True), "s_dve_c")
                            p.w(v, "s_dve_c", 4 * u + 1)
                            p.inc(v.scalar_tensor_tensor(
                                out=pd, in0=pd, scalar=1.0, in1=etril[:],
                                op0=ALU.mult, op1=ALU.mult,
                                accum_out=Zp[u % 2][:, ncol + 1:ncol + 2]),
                                "s_dve_c")
                            p.w(v, "s_dve_c", 4 * u + 2)
                            p.inc(v.reduce_sum(Zs[u % 2][:],
                                               Zp[u % 2][:, 0:ncol + 2],
                                               axis=AX.X), "s_dve_c")
                            p.w(v, "s_dve_c", 4 * u + 3)
                            p.inc(v.reciprocal(rz[u % 2][:], Zs[u % 2][:]),
                                  "s_dve_c")
                            p.w(v, "s_dve_c", 4 * u + 4)
                            p.inc(v.tensor_scalar(out=p_sb[u % 3]()[:],
                                                  in0=p_sb[u % 3]()[:],
                                                  scalar1=rz[u % 2][:],
                                                  scalar2=None,
                                                  op0=ALU.mult), "s_dve_p")
                block.vector(phC_dve)
            esB2.close()

            if stop_after == "C":
                esBIG.close()
                return nc
            full_barrier(skip_sync=True)
            with ExitStack() as esD:
                Dm = lambda n, shp, dt: esD.enter_context(nc.sbuf_tensor(n, shp, dt))
                xblk = [Dm(f"xblk{i}", [128, T], F32) for i in range(2)]
                tmpD = Dm("tmpD", [128, T], F32)
                r1b = [Dm(f"r1b{i}", [128, TH], F32) for i in range(2)]
                r1acc = xn1[:].bitcast(F32)
                pso = [esD.enter_context(nc.psum_tensor(f"pso{i}", [128, T], F32))
                       for i in range(2)]
                bD = dict(p.cnt)
                ldw_bD = [p.cnt["ld_w0"], p.cnt["ld_w1"]]
                ldx_bD = [p.cnt["ld_x0"], p.cnt["ld_x1"]]

                def phD_sync(sync):
                    def wdma(i):
                        p.w(sync, "s_pool", bD["s_pool"] + i - 1)
                        p.inc(sync.dma_start(
                            out=wstg[i % 2][:].rearrange("p (c i) -> p c i", i=128),
                            in_=woT[:, i * 128:(i + 1) * 128]
                                .rearrange("(c p) i -> p c i", p=128)),
                            f"ld_w{i % 2}", 16)
                    wdma(0)
                    wdma(1)
                    for i in range(NCC):
                        if i == 0:
                            p.w(sync, "s_pe_pv", NKV)
                        p.w(sync, "s_dve", bD["s_dve"] + i - 1)
                        p.inc(sync.dma_start(out=xblk[i % 2][:],
                                             in_=xT[i * 128:(i + 1) * 128, :]),
                              f"ld_x{i % 2}", 16)
                        if i + 2 < NCC:
                            wdma(i + 2)
                block.sync(phD_sync)

                def phD_pool(g):
                    for i in range(NCC):
                        p.w(g, "s_dve", bD["s_dve"] + i + 1)
                        p.inc(g.dma_start(out=r1T[i * 128:(i + 1) * 128, :],
                                          in_=r1b[i % 2][:]), f"st_r1{i % 2}", 16)
                block.gpsimd(phD_pool)

                def phD_pe(t):
                    p.w(t, "s_dve_ev", NU)
                    p.w(t, "s_act_af", NKV)
                    for i in range(NCC):
                        p.w(t, "s_pool", bD["s_pool"] + i + 1)
                        p.w(t, "s_act", bD["s_act"] + i - 1)
                        for ca in range(NCC):
                            for ns in range(NS):
                                ins = t.matmul(
                                    pso[i % 2][:, ns * SW:(ns + 1) * SW],
                                    wbf[i % 2][:, ca * 128:(ca + 1) * 128],
                                    attfm[:, ca * T + ns * SW:
                                          ca * T + (ns + 1) * SW],
                                    start=(ca == 0), stop=(ca == NCC - 1))
                        p.inc(ins, "s_pe")
                block.tensor(phD_pe)

                def phD_act(s):
                    for i in range(NCC):
                        p.w(s, "s_pe", bD["s_pe"] + i + 1)
                        p.w(s, "s_dve", bD["s_dve"] + i)
                        p.inc(s.activation(tmpD[:], pso[i % 2][:], AF.Identity,
                                           bias=bof[:, i:i + 1]), "s_act")
                block.scalar(phD_act)

                def phD_dve(v):
                    def radd(i):
                        p.w(v, "s_act", bD["s_act"] + i + 1)
                        p.w(v, f"ld_x{i % 2}", ldx_bD[i % 2] + 16 * (i // 2 + 1))
                        p.w(v, f"st_r1{i % 2}", 16 * (i // 2))
                        v.tensor_add(r1acc[:, i * TH:(i + 1) * TH],
                                     tmpD[:, 0:TH], xblk[i % 2][:, 0:TH])
                        p.inc(v.tensor_add(r1b[i % 2][:], tmpD[:, TH:T],
                                           xblk[i % 2][:, TH:T]), "s_dve")
                    for i in range(NCC):
                        p.w(v, f"ld_w{i % 2}", ldw_bD[i % 2] + 16 * (i // 2 + 1))
                        p.w(v, "s_pe", bD["s_pe"] + i - 1)
                        p.inc(v.tensor_copy(wbf[i % 2][:], wstg[i % 2][:]),
                              "s_pool")
                        if i >= 1:
                            radd(i - 1)
                    radd(NCC - 1)
                block.vector(phD_dve)


            esW.close()
            full_barrier()
            with ExitStack() as esE:
                Em = lambda n, shp, dt: esE.enter_context(nc.sbuf_tensor(n, shp, dt))
                r1acc = xn1[:].bitcast(F32)
                xn2 = qbuf[:, 0:NCC * TH]
                hblk = qbuf[:, NCC * TH:NCC * TH + HBS * TH]
                p1stg = [Em(f"p1stg{i}", [128, C], F32) for i in range(2)]
                p1bf = [Em(f"p1bf{i}", [128, C], BF16) for i in range(2)]
                p2stg = [Em(f"p2stg{i}", [128, HBS * 128], F32) for i in range(2)]
                p2bf = [Em(f"p2bf{i}", [128, HBS * 128], BF16) for i in range(2)]
                sq2 = [Em(f"sq2{i}", [128, TH], BF16) for i in range(2)]
                mb2 = Em("mb2", [128, TH], F32)
                t22 = Em("t22", [128, TH], F32)
                t32 = Em("t32", [128, TH], BF16)
                ab2 = Em("ab2", [128, TH], BF16)
                bma2 = Em("bma2", [128, TH], BF16)
                outsb = [mb2, t22]

                pf_p1 = [0, 0]
                for hf in range(2):
                    if hf == 1:
                        pf_p1 = [p.cnt["ld_p10"], p.cnt["ld_p11"]]

                        def phE_prefetch(sync):
                            p.w(sync, "s_pool", p.cnt["s_pool"])
                            for hc in range(2):
                                p.inc(sync.dma_start(
                                    out=p1stg[hc % 2][:]
                                    .rearrange("p (c i) -> p c i", i=128),
                                    in_=p1T[:, hc * 128:(hc + 1) * 128]
                                    .rearrange("(c p) i -> p c i", p=128)),
                                    f"ld_p1{hc % 2}", 16)
                        block.sync(phE_prefetch)
                    full_barrier(skip_sync=True)
                    tofs = hf * TH
                    bE = dict(p.cnt)

                    if hf == 1:
                        def phE0_sync(sync, bE=bE):
                            p.w(sync, "s_dve", bE["s_dve"])
                            for sl in range(2):
                                p.w(sync, f"st_r1{sl}", 16 * (NCC // 2))
                            p.inc(sync.dma_start(
                                out=r1acc[:].rearrange("p (c t) -> p c t", t=TH),
                                in_=r1T.rearrange("(c p) t -> p c t", p=128)),
                                "ld_r1", 16)
                        block.sync(phE0_sync)

                    with ExitStack() as esE1:
                        psS2 = esE1.enter_context(
                            nc.psum_tensor(f"psS2_{hf}", [128, TH], F32))
                        psSS2 = esE1.enter_context(
                            nc.psum_tensor(f"psSS2_{hf}", [128, TH], F32))

                        def phE1_cast(v, hf=hf, bE=bE):
                            for c in range(NCC):
                                if hf == 1:
                                    p.w(v, "ld_r1", 16)
                                cs = slice(c * TH, (c + 1) * TH)
                                p.inc(v.tensor_copy(xn2[:, cs], r1acc[:, cs]),
                                      "s_pool")
                        block.vector(phE1_cast)

                        def phE1_sq(a, hf=hf, bE=bE):
                            for c in range(NCC):
                                if hf == 1:
                                    p.w(a, "ld_r1", 16)
                                p.w(a, "s_pe", bE["s_pe"] + c - 1)
                                cs = slice(c * TH, (c + 1) * TH)
                                p.inc(a.activation(sq2[c % 2][:], r1acc[:, cs],
                                                   AF.Square), "s_act")
                        block.scalar(phE1_sq)

                        def phE1_pe(t, bE=bE):
                            p.w(t, "s_dve", bE["s_dve"])
                            p.w(t, "s_act", bE["s_act"])
                            for c in range(NCC):
                                p.w(t, "s_pool", bE["s_pool"] + c + 1)
                                p.w(t, "s_act", bE["s_act"] + c + 1)
                                for ns in range(NSH):
                                    sl = slice(ns * SWT, (ns + 1) * SWT)
                                    t.matmul(psS2[:, sl], ones128[:],
                                             xn2[:, c * TH + ns * SWT:
                                                 c * TH + (ns + 1) * SWT],
                                             start=(c == 0), stop=(c == NCC - 1))
                                    ins = t.matmul(psSS2[:, sl], ones128[:],
                                                   sq2[c % 2][:, sl],
                                                   start=(c == 0),
                                                   stop=(c == NCC - 1))
                                p.inc(ins, "s_pe")
                        block.tensor(phE1_pe)

                        def phE2_v1(v, bE=bE):
                            p.w(v, "s_pe", bE["s_pe"] + NCC)
                            p.inc(v.tensor_scalar(out=mb2[:], in0=psS2[:],
                                                  scalar1=1.0 / C, scalar2=None,
                                                  op0=ALU.mult), "s_dve")
                        block.vector(phE2_v1)

                        def phE2_s1(s):
                            p.w(s, "s_dve", p.cnt["s_dve"])
                            p.inc(s.activation(t22[:], mb2[:], AF.Square), "s_act")
                        block.scalar(phE2_s1)

                        def phE2_v2(v):
                            p.w(v, "s_act", p.cnt["s_act"])
                            p.inc(v.scalar_tensor_tensor(
                                out=t22[:], in0=psSS2[:], scalar=1.0 / C,
                                in1=t22[:], op0=ALU.mult, op1=ALU.subtract),
                                "s_dve")
                        block.vector(phE2_v2)

                        def phE2_s2(s):
                            p.w(s, "s_dve", p.cnt["s_dve"])
                            p.inc(s.activation(t22[:], t22[:], AF.Ln,
                                               bias=f32(EPS)), "s_act")
                            p.w(s, "s_act", p.cnt["s_act"])
                            p.inc(s.activation(ab2[:], t22[:], AF.Exp,
                                               scale=f32(-0.5)), "s_act")
                        block.scalar(phE2_s2)

                        def phE2_v3(v):
                            p.w(v, "s_act", p.cnt["s_act"])
                            p.inc(v.scalar_tensor_tensor(
                                out=bma2[:], in0=mb2[:], scalar=-1.0, in1=ab2[:],
                                op0=ALU.mult, op1=ALU.mult), "s_dve")
                            for c in range(NCC):
                                cs = slice(c * TH, (c + 1) * TH)
                                p.w(v, "s_dve", p.cnt["s_dve"])
                                p.inc(v.tensor_mul(t32[:], xn2[:, cs], ab2[:]),
                                      "s_dve")
                                p.w(v, "s_dve", p.cnt["s_dve"])
                                p.inc(v.tensor_add(xn2[:, cs], t32[:], bma2[:]),
                                      "s_dve")
                        block.vector(phE2_v3)

                    full_barrier(skip_sync=True)
                    with ExitStack() as esE4:
                        psh = [esE4.enter_context(
                            nc.psum_tensor(f"psh{hf}_{i}", [128, TH], F32))
                            for i in range(2)]
                        pso2 = [esE4.enter_context(
                            nc.psum_tensor(f"pso2{hf}_{i}", [128, TH], F32))
                            for i in range(2)]
                        bE4 = dict(p.cnt)
                        stob = [p.cnt["st_o0"], p.cnt["st_o1"]]
                        b2f_ap = lambda i: b2f[:, i:i + 1]
                        ldp1 = (pf_p1 if hf == 1
                                else [p.cnt["ld_p10"], p.cnt["ld_p11"]])
                        ldp2 = [p.cnt["ld_p20"], p.cnt["ld_p21"]]

                        def phE4_sync(sync, bE4=bE4, hf=hf):
                            for hb in range(NHB):
                                for hc in range(HBS):
                                    if hf == 1 and hb == 0 and hc < 2:
                                        continue
                                    if hf == 0 and hb == 0 and hc < 2:
                                        p.w(sync, "s_pe_pv", NKV)
                                    else:
                                        p.w(sync, "s_pool",
                                            bE4["s_pool"] + hb * (HBS + NCC)
                                            + hc - 1)
                                    hcg = hb * HBS + hc
                                    p.inc(sync.dma_start(
                                        out=p1stg[hc % 2][:]
                                        .rearrange("p (c i) -> p c i", i=128),
                                        in_=p1T[:, hcg * 128:(hcg + 1) * 128]
                                        .rearrange("(c p) i -> p c i", p=128)),
                                        f"ld_p1{hc % 2}", 16)
                                for i in range(NCC):
                                    if hf == 0 and hb == 0 and i < 2:
                                        p.w(sync, "s_pe_pv", NKV)
                                    else:
                                        p.w(sync, "s_pool",
                                            bE4["s_pool"] + hb * (HBS + NCC)
                                            + HBS + i - 1)
                                    p.inc(sync.dma_start(
                                        out=p2stg[i % 2][:]
                                        .rearrange("p (c j) -> p c j", j=128),
                                        in_=p2T[hb * HBS * 128:
                                                (hb + 1) * HBS * 128,
                                                i * 128:(i + 1) * 128]
                                        .rearrange("(c p) j -> p c j", p=128)),
                                        f"ld_p2{i % 2}", 16)
                        block.sync(phE4_sync)

                        def phE4_cast(v, bE4=bE4, ldp1=ldp1, ldp2=ldp2):
                            pend = []
                            for hb in range(NHB):
                                for hc in range(HBS):
                                    nf = hb * (HBS // 2) + hc // 2 + 1
                                    p.w(v, f"ld_p1{hc % 2}",
                                        ldp1[hc % 2] + 16 * nf)
                                    p.w(v, "s_pe",
                                        bE4["s_pe"] + hb * (HBS + NCC) + hc - 1)
                                    p.inc(v.tensor_copy(p1bf[hc % 2][:],
                                                        p1stg[hc % 2][:]),
                                          "s_pool")
                                    if hc < len(pend):
                                        pend[hc]()
                                pend = []

                                def evac2(i, hb=hb):
                                    p.w(v, "s_pe",
                                        bE4["s_pe"] + hb * (HBS + NCC)
                                        + HBS + i + 1)
                                    p.w(v, "s_dve",
                                        bE4["s_dve"] + hb * NCC + i)
                                    cs = slice(i * TH, (i + 1) * TH)
                                    if hb == NHB - 1:
                                        p.w(v, f"st_o{i % 2}",
                                            stob[i % 2] + 16 * (i // 2))
                                        p.inc(v.scalar_tensor_tensor(
                                            out=outsb[i % 2][:],
                                            in0=pso2[i % 2][:],
                                            scalar=b2f_ap(i),
                                            in1=r1acc[:, cs],
                                            op0=ALU.add, op1=ALU.add), "s_dve")
                                    else:
                                        p.inc(v.tensor_add(r1acc[:, cs],
                                                           pso2[i % 2][:],
                                                           r1acc[:, cs]), "s_dve")
                                for i in range(NCC):
                                    nf = hb * (NCC // 2) + i // 2 + 1
                                    p.w(v, f"ld_p2{i % 2}",
                                        ldp2[i % 2] + 16 * nf)
                                    p.w(v, "s_pe",
                                        bE4["s_pe"] + hb * (HBS + NCC)
                                        + HBS + i - 1)
                                    p.inc(v.tensor_copy(p2bf[i % 2][:],
                                                        p2stg[i % 2][:]),
                                          "s_pool")
                                    if i >= 2:
                                        evac2(i - 2)
                                if hb == NHB - 1:
                                    evac2(NCC - 2)
                                    evac2(NCC - 1)
                                else:
                                    pend = [lambda i=NCC - 2, hb=hb: evac2(i, hb),
                                            lambda i=NCC - 1, hb=hb: evac2(i, hb)]
                        block.vector(phE4_cast)

                        def phE4_pe(t, bE4=bE4):
                            for hb in range(NHB):
                                for hc in range(HBS):
                                    p.w(t, "s_pool",
                                        bE4["s_pool"] + hb * (HBS + NCC) + hc + 1)
                                    p.w(t, "s_act",
                                        bE4["s_act"] + hb * HBS + hc - 1)
                                    for c in range(NCC):
                                        for ns in range(NSH):
                                            sl = slice(ns * SWT, (ns + 1) * SWT)
                                            ins = t.matmul(
                                                psh[hc % 2][:, sl],
                                                p1bf[hc % 2][:,
                                                             c * 128:(c + 1) * 128],
                                                xn2[:, c * TH + ns * SWT:
                                                    c * TH + (ns + 1) * SWT],
                                                start=(c == 0),
                                                stop=(c == NCC - 1))
                                    p.inc(ins, "s_pe")
                                for i in range(NCC):
                                    p.w(t, "s_pool",
                                        bE4["s_pool"] + hb * (HBS + NCC)
                                        + HBS + i + 1)
                                    p.w(t, "s_act",
                                        bE4["s_act"] + (hb + 1) * HBS)
                                    p.w(t, "s_dve",
                                        bE4["s_dve"] + hb * NCC + i - 1)
                                    for hc in range(HBS):
                                        for ns in range(NSH):
                                            sl = slice(ns * SWT, (ns + 1) * SWT)
                                            ins = t.matmul(
                                                pso2[i % 2][:, sl],
                                                p2bf[i % 2][:,
                                                            hc * 128:(hc + 1) * 128],
                                                hblk[:, hc * TH + ns * SWT:
                                                     hc * TH + (ns + 1) * SWT],
                                                start=(hc == 0),
                                                stop=(hc == HBS - 1))
                                    p.inc(ins, "s_pe")
                        block.tensor(phE4_pe)

                        def phE4_act(s, bE4=bE4):
                            for hb in range(NHB):
                                for hc in range(HBS):
                                    p.w(s, "s_pe",
                                        bE4["s_pe"] + hb * (HBS + NCC) + hc + 1)
                                    p.inc(s.activation(
                                        hblk[:, hc * TH:(hc + 1) * TH],
                                        psh[hc % 2][:], AF.Gelu,
                                        bias=b1f[:, hb * HBS + hc:
                                                 hb * HBS + hc + 1]), "s_act")
                        block.scalar(phE4_act)


                        def phE5_st(g, bE4=bE4, stob=stob, tofs=tofs):
                            for i in range(NCC):
                                p.w(g, "s_dve",
                                    bE4["s_dve"] + (NHB - 1) * NCC + i + 1)
                                p.inc(g.dma_start(
                                    out=outT[i * 128:(i + 1) * 128,
                                             tofs:tofs + TH],
                                    in_=outsb[i % 2][:]), f"st_o{i % 2}", 16)
                        block.gpsimd(phE5_st)

                def fin(sync):
                    p.w(sync, "st_o0", p.cnt["st_o0"])
                    p.w(sync, "st_o1", p.cnt["st_o1"])
                block.sync(fin)
        esBIG.close()
    return nc


def _host_tables(H):
    D = 128
    inv = 1.0 / 10000.0 ** (np.arange(0, D, 2, dtype=np.float64) / D)
    hidx = np.arange(H, dtype=np.float64)
    freqs = np.outer(hidx, inv)
    emb = np.concatenate([freqs, freqs], axis=1)
    cos_t = np.ascontiguousarray(np.cos(emb).T.astype(np.float32))
    sin = np.sin(emb).T.astype(np.float32)
    ssin = sin.copy()
    ssin[:64] = -sin[:64]
    ssin = np.ascontiguousarray(ssin)
    tril = np.tril(np.ones((128, 128), dtype=np.float32))
    etril = np.where(tril > 0, np.float32(np.e),
                     np.float32(1.0)).astype(np.float32)
    ident = np.eye(128, dtype=np.float32)
    return cos_t, ssin, etril, ident


def kernel(L0_attn_x, L0_attn_wq, L0_attn_bq, L0_attn_wk, L0_attn_bk,
           L0_attn_wv, L0_attn_bv, L0_attn_wo, L0_attn_bo,
           L0_ffn_proj1, L0_ffn_proj1_bias, L0_ffn_proj2, L0_ffn_proj2_bias,
           L0_ffn_gate_w):
    x = np.asarray(L0_attn_x)
    Bn, T, C = x.shape
    H = 16
    KVH = 4
    HF = np.asarray(L0_ffn_proj1).shape[0]
    KVD = KVH * 128
    assert C == H * 128

    cos_t, ssin_t, etril, ident = _host_tables(H)

    def fm(b):
        return np.ascontiguousarray(
            np.asarray(b).reshape(-1, 128).T.astype(np.float32))

    common = {
        "wqT": np.ascontiguousarray(np.asarray(L0_attn_wq).T),
        "wkT": np.ascontiguousarray(np.asarray(L0_attn_wk).T),
        "wvT": np.ascontiguousarray(np.asarray(L0_attn_wv).T),
        "woT": np.ascontiguousarray(np.asarray(L0_attn_wo).T),
        "p1T": np.ascontiguousarray(np.asarray(L0_ffn_proj1).T),
        "p2T": np.ascontiguousarray(np.asarray(L0_ffn_proj2).T),
        "bq_fm": fm(L0_attn_bq),
        "bk_fm": fm(L0_attn_bk),
        "bv_row": np.ascontiguousarray(
            np.asarray(L0_attn_bv).reshape(1, KVD).astype(np.float32)),
        "bo_fm": fm(L0_attn_bo),
        "b1_fm": fm(L0_ffn_proj1_bias),
        "b2_fm": fm(L0_ffn_proj2_bias),
        "cos_t": cos_t, "ssin_t": ssin_t, "etril": etril, "ident": ident,
        "gate_w": np.ascontiguousarray(
            np.asarray(L0_ffn_gate_w).reshape(-1, 128, 8)
            .transpose(1, 0, 2).reshape(128, -1).astype(np.float32)),
    }
    in_maps = []
    for b in range(Bn):
        m = dict(common)
        m["xT"] = np.ascontiguousarray(x[b].T)
        in_maps.append(m)

    nc = build_nc(T=T, H=H, KVH=KVH, HF=HF)
    res = run_bass_kernel_spmd(nc, in_maps, core_ids=list(range(Bn)))
    out = np.empty((Bn, T, C), dtype=np.float32)
    for b in range(Bn):
        out[b] = res.results[b]["outT"].T
    return out
